# version 10
# baseline (speedup 1.0000x reference)
import contextlib
import numpy as np

try:
    import concourse.bass as bass
except ImportError:
    import sys
    sys.path.insert(0, "/opt/trn_rl_repo")
    import concourse.bass as bass
import concourse.mybir as mybir
from concourse.bass_utils import run_bass_kernel_spmd

F32 = mybir.dt.float32
BF16 = mybir.dt.bfloat16
I32 = mybir.dt.int32
AF = mybir.ActivationFunctionType
ALU = mybir.AluOpType
AX = mybir.AxisListType

D = 1024
F = 4096
H = 16
HD = 64
NC = 8
FC = 32
T = 256
QG = 512
EPS = 1e-5
NEG = -1.0e30


class KB:

    def __init__(self, nc, es):
        self.nc = nc
        self.es = es
        self.eng = dict(PE=nc.tensor, ACT=nc.scalar, DVE=nc.vector,
                        POOL=nc.gpsimd, SP=nc.sync)
        self.sem = {k: es.enter_context(nc.semaphore("e_" + k)) for k in self.eng}
        self.cnt = {k: 0 for k in self.eng}
        self.wr = {}
        self.rd = {}
        self.waited = {}
        self.semobj = {k: self.sem[k] for k in self.eng}
        self.streams = {}

    def _wait(self, eng, semkey, val):
        if semkey == eng:
            return
        if semkey in self.streams:
            val = 16 * self.streams[semkey].issued
        else:
            assert val is not None and val <= self.cnt[semkey], (semkey, val)
        w = (eng, semkey)
        if self.waited.get(w, -1) >= val:
            return
        self.eng[eng].wait_ge(self.semobj[semkey], val)
        self.waited[w] = val

    def dep(self, eng, rkeys, wkeys):
        for k in rkeys:
            if k in self.wr:
                s, v = self.wr[k]
                self._wait(eng, s, v)
        for k in wkeys:
            if k in self.wr:
                s, v = self.wr[k]
                self._wait(eng, s, v)
            for s, v in self.rd.get(k, {}).items():
                self._wait(eng, s, v)

    def _note(self, eng_semkey, val, rkeys, wkeys):
        for k in wkeys:
            self.wr[k] = (eng_semkey, val)
            self.rd[k] = {}
        for k in rkeys:
            self.rd.setdefault(k, {})[eng_semkey] = val

    @staticmethod
    def keys(aps):
        return [a.name for a in aps]

    def op(self, eng, fn, reads, writes):
        rk, wk = self.keys(reads), self.keys(writes)
        self.dep(eng, rk, wk)
        ins = fn(self.eng[eng])
        self.cnt[eng] += 1
        ins.then_inc(self.sem[eng], 1)
        self._note(eng, self.cnt[eng], rk, wk)
        return ins

    def mms(self, items):
        rk, wk = [], []
        for o, l, r, st, sp in items:
            rk += self.keys([l, r])
            wk.append(o.name)
        self.dep("PE", rk, wk)
        for o, l, r, st, sp in items:
            ins = self.eng["PE"].matmul(out=o, lhsT=l, rhs=r, start=st, stop=sp)
        self.cnt["PE"] += 1
        ins.then_inc(self.sem["PE"], 1)
        self._note("PE", self.cnt["PE"], rk, wk)

    def barrier(self):
        self.nc.all_engine_barrier()
        self.wr = {k: v for k, v in self.wr.items() if v[0] in self.streams}
        self.rd = {k: {s: v for s, v in d.items() if s in self.streams}
                   for k, d in self.rd.items()}
        self.rd = {k: d for k, d in self.rd.items() if d}
        self.waited = {}

    def stream(self, name):
        s = DmaStream(self, name, self.es)
        self.streams[name] = s
        self.semobj[name] = s.sem
        return s


class DmaStream:
    def __init__(self, kb, name, es):
        self.kb = kb
        self.name = name
        self.sem = es.enter_context(kb.nc.semaphore("dma_" + name))
        self.issued = 0

    def dma(self, out, in_, **kw):
        kb = self.kb
        rk, wk = [in_.name], [out.name]
        kb.dep("SP", rk, wk)
        ins = kb.eng["SP"].dma_start(out=out, in_=in_, **kw)
        ins.then_inc(self.sem, 16)
        self.issued += 1
        kb._note(self.name, None, rk, wk)

    def wait_all(self, eng):
        self.kb._wait(eng, self.name, None)


def build_nc(S):
    nT = S // T
    nG = S // QG
    nKB = S // 128
    nc = bass.Bass("TRN2", target_bir_lowering=False,
                   detect_race_conditions=False)

    dt_in = lambda n, sh, dt: nc.dram_tensor(n, sh, dt, kind="ExternalInput").ap()
    dt_int = lambda n, sh, dt: nc.dram_tensor(n, sh, dt, kind="Internal").ap()

    xT = dt_in("xT", [D, S], F32)
    Wq, Wk, Wv, Wo = (dt_in(n, [D, D], F32) for n in ("Wq", "Wk", "Wv", "Wo"))
    bq, bk, bv, bo = (dt_in(n, [D], F32) for n in ("bq", "bk", "bv", "bo"))
    agam, abet = dt_in("agam", [D], F32), dt_in("abet", [D], F32)
    lgam, lbet = dt_in("lgam", [D], F32), dt_in("lbet", [D], F32)
    fc1q = dt_in("fc1q", [D, F], I32)
    fc1s = dt_in("fc1s", [NC, F], F32)
    fc1b = dt_in("fc1b", [F], F32)
    fc2q = dt_in("fc2q", [F, D], I32)
    fc2s = dt_in("fc2s", [FC, D], F32)
    fc2b = dt_in("fc2b", [D], F32)
    outT = nc.dram_tensor("outT", [D, S], F32, kind="ExternalOutput").ap()

    x1d = dt_int("x1d", [D, S], F32)
    u2d = dt_int("u2d", [D, S], BF16)
    x2d = dt_int("x2d", [D, S], F32)
    qTd = dt_int("qTd", [D, S], BF16)
    vd = dt_int("vd", [S, 1040], BF16)
    oTd = dt_int("oTd", [D, S], BF16)
    sumd = dt_int("sumd", [H, S], F32)
    w1d = dt_int("w1d", [128, NC * F], BF16)
    w2d = dt_int("w2d", [128, FC * D], BF16)

    import ml_dtypes
    mask_np = np.where(np.arange(128)[:, None] > np.arange(128)[None, :],
                       np.float32(NEG), np.float32(0.0))
    maskc = nc.inline_tensor(mask_np.astype(ml_dtypes.bfloat16), name="maskc").ap()
    id128c = nc.inline_tensor(np.eye(128, dtype=np.float32)
                              .astype(ml_dtypes.bfloat16), name="id128c").ap()
    sel_np = np.zeros((128, NC * 16), dtype=np.float32)
    for c in range(NC):
        sel_np[0:64, c * 16 + 2 * c] = 1.0
        sel_np[64:128, c * 16 + 2 * c + 1] = 1.0
    selc = nc.inline_tensor(sel_np.astype(ml_dtypes.bfloat16), name="selc").ap()
    selh_np = np.zeros((16, NC * 128), dtype=np.float32)
    for c in range(NC):
        selh_np[2 * c, c * 128:c * 128 + 64] = 1.0
        selh_np[2 * c + 1, c * 128 + 64:(c + 1) * 128] = 1.0
    selhc = nc.inline_tensor(selh_np.astype(ml_dtypes.bfloat16), name="selhc").ap()
    id16 = nc.inline_tensor(np.eye(16, dtype=np.float32), name="id16").ap()

    def fm(dram):
        return dram.rearrange("(c p) s -> p c s", p=128)

    es0 = contextlib.ExitStack()
    with es0:
        kb = KB(nc, es0)
        sb = lambda n, sh, dt: es0.enter_context(nc.sbuf_tensor(n, sh, dt))

        mask_sb = sb("mask_sb", [128, 128], BF16)
        id128_sb = sb("id128_sb", [128, 128], BF16)
        sel_sb = sb("sel_sb", [128, NC * 16], BF16)
        id16_sb = sb("id16_sb", [16, 16], F32)
        selh_sb = sb("selh_sb", [16, NC * 128], BF16)
        lgam_c = sb("lgam_c", [128, NC], F32)
        lbet_c = sb("lbet_c", [128, NC], F32)
        agam_c = sb("agam_c", [128, NC], F32)
        abet_c = sb("abet_c", [128, NC], F32)
        bq_c = sb("bq_c", [128, NC], F32)
        bk_c = sb("bk_c", [128, NC], F32)
        bo_c = sb("bo_c", [128, NC], F32)
        b1_c = sb("b1_c", [128, FC], F32)
        b2_c = sb("b2_c", [128, NC], F32)
        onesP = sb("onesP", [128, 128], BF16)
        eps_c = sb("eps_c", [128, 1], F32)
        ones_f = sb("ones_f", [1, 128], F32)
        quart_f = sb("quart_f", [1, 128], F32)
        ones_b = sb("ones_b", [1, 128], BF16)
        qn2run = sb("qn2run", [16, 1], F32)
        kn2run = sb("kn2run", [16, 1], F32)
        negbrow = sb("negbrow", [1, 16], F32)
        negb_c = sb("negb_c", [128, 8], F32)
        stat = {n: sb("st_" + n, [128, T], F32)
                for n in ("mean", "v", "r", "mean2", "v2", "r2")}

        ld0 = kb.stream("ld0")
        colv = lambda v: v.rearrange("(m p) -> p m", p=128)
        for dst, src in ((lgam_c, lgam), (lbet_c, lbet), (agam_c, agam),
                         (abet_c, abet), (bq_c, bq), (bk_c, bk), (bo_c, bo),
                         (b1_c, fc1b), (b2_c, fc2b)):
            ld0.dma(dst[:], colv(src), allow_slow_non_contiguous=True)
        ld0.dma(mask_sb[:], maskc)
        ld0.dma(id128_sb[:], id128c)
        ld0.dma(sel_sb[:], selc)
        ld0.dma(selh_sb[:], selhc)
        ld0.dma(id16_sb[:], id16)
        kb.op("DVE", lambda e: e.memset(onesP[:], 1.0 / 1024.0), [], [onesP[:]])
        kb.op("DVE", lambda e: e.memset(eps_c[:], EPS), [], [eps_c[:]])
        kb.op("DVE", lambda e: e.memset(ones_f[:], 1.0), [], [ones_f[:]])
        kb.op("DVE", lambda e: e.memset(quart_f[:], 0.25), [], [quart_f[:]])
        kb.op("DVE", lambda e: e.memset(ones_b[:], 1.0), [], [ones_b[:]])
        ld0.wait_all("DVE")
        kb.op("DVE", lambda e: e.tensor_scalar_mul(out=b2_c[:], in0=b2_c[:],
                                                   scalar1=0.25),
              [b2_c[:]], [b2_c[:]])

        def ln_stats(src, xb, sq, pstat, sfx):
            mean, var, rstd = (stat[k + ("" if sfx == 1 else "2")]
                               for k in ("mean", "v", "r"))
            kb.op("DVE", lambda e: e.tensor_copy(out=xb[:], in_=src[:]),
                  [src[:]], [xb[:]])
            kb.op("DVE", lambda e: e.tensor_mul(out=sq[:], in0=xb[:], in1=xb[:]),
                  [xb[:]], [sq[:]])
            kb.mms([(pstat[:, 0:T], onesP[:], xb[:, c * T:(c + 1) * T],
                     c == 0, c == NC - 1) for c in range(NC)])
            kb.mms([(pstat[:, T:2 * T], onesP[:], sq[:, c * T:(c + 1) * T],
                     c == 0, c == NC - 1) for c in range(NC)])
            kb.op("DVE", lambda e: e.tensor_copy(out=mean[:], in_=pstat[:, 0:T]),
                  [pstat[:]], [mean[:]])
            kb.op("DVE", lambda e: e.tensor_mul(out=var[:], in0=mean[:], in1=mean[:]),
                  [mean[:]], [var[:]])
            kb.op("DVE", lambda e: e.tensor_tensor(out=var[:], in0=pstat[:, T:2 * T],
                                                   in1=var[:], op=ALU.subtract),
                  [pstat[:], var[:]], [var[:]])
            kb.op("ACT", lambda e: e.activation(out=rstd[:], in_=var[:], func=AF.Sqrt,
                                                bias=eps_c[:], scale=1.0),
                  [var[:], eps_c[:]], [rstd[:]])
            kb.op("DVE", lambda e: e.reciprocal(out=rstd[:], in_=rstd[:]),
                  [rstd[:]], [rstd[:]])
            return mean, rstd

        def ln_stats_pre(xb_t, sq, pstat, sfx):
            mean, var, rstd = (stat[k + ("" if sfx == 1 else "2")]
                               for k in ("mean", "v", "r"))
            kb.mms([(pstat[:, 0:T], onesP[:], xb_t[:, c * T:(c + 1) * T],
                     c == 0, c == NC - 1) for c in range(NC)])
            kb.mms([(pstat[:, T:2 * T], onesP[:], sq[:, c * T:(c + 1) * T],
                     c == 0, c == NC - 1) for c in range(NC)])
            kb.op("DVE", lambda e: e.tensor_copy(out=mean[:], in_=pstat[:, 0:T]),
                  [pstat[:]], [mean[:]])
            kb.op("DVE", lambda e: e.tensor_mul(out=var[:], in0=mean[:], in1=mean[:]),
                  [mean[:]], [var[:]])
            kb.op("DVE", lambda e: e.tensor_tensor(out=var[:], in0=pstat[:, T:2 * T],
                                                   in1=var[:], op=ALU.subtract),
                  [pstat[:], var[:]], [var[:]])
            kb.op("ACT", lambda e: e.activation(out=rstd[:], in_=var[:], func=AF.Sqrt,
                                                bias=eps_c[:], scale=1.0),
                  [var[:], eps_c[:]], [rstd[:]])
            kb.op("DVE", lambda e: e.reciprocal(out=rstd[:], in_=rstd[:]),
                  [rstd[:]], [rstd[:]])
            return mean, rstd

        def bc(ap):
            return ap.rearrange("p (o s) -> p o s", o=1).broadcast_to([128, NC, T])

        def c3(ap):
            return ap.rearrange("p (c s) -> p c s", s=T)

        def ffn_phase(src_dram, dst_dram, w1b, w2b, ph):
            esA = contextlib.ExitStack()
            with esA:
                sbA = lambda n, sh, dt: esA.enter_context(
                    nc.sbuf_tensor(n + ph, sh, dt))
                psA = lambda n, sh: esA.enter_context(
                    nc.psum_tensor(n + ph, sh, F32))
                x_t = [sbA("x_t%d" % i, [128, NC * T], F32) for i in (0, 1)]
                u_t = sbA("u_t", [128, NC * T], BF16)
                hf = sbA("hf", [128, NC * T], F32)
                h_b = [sbA("h_b%d" % i, [128, NC * T], BF16) for i in (0, 1)]
                xb = sbA("xb", [128, NC * T], BF16)
                sq = sbA("sq", [128, NC * T], BF16)
                y_b = sbA("y_b", [128, FC * T], BF16)
                pstat = [psA("pstat%d" % i, [128, 512]) for i in (0, 1)]
                py = [psA("py%d" % i, [128, 512]) for i in (0, 1)]
                pz = [psA("pz%d" % i, [128, 512]) for i in (0, 1)]
                lds = [kb.stream("ldx%d" % i + ph) for i in (0, 1)]
                sts = [kb.stream("stx%d" % i + ph) for i in (0, 1)]
                srcv, dstv = fm(src_dram), fm(dst_dram)

                def load(t):
                    lds[t % 2].dma(c3(x_t[t % 2][:]), srcv[:, :, t * T:(t + 1) * T])

                def p1a(t):
                    x = x_t[t % 2]
                    lds[t % 2].wait_all("POOL")
                    mean, rstd = ln_stats(x[:], xb, sq, pstat[0], 1)
                    kb.op("DVE", lambda e: e.tensor_tensor(
                        out=c3(u_t[:]), in0=c3(x[:]), in1=bc(mean[:]),
                        op=ALU.subtract), [x[:], mean[:]], [u_t[:]])
                    kb.op("DVE", lambda e: e.tensor_tensor(
                        out=c3(u_t[:]), in0=c3(u_t[:]), in1=bc(rstd[:]),
                        op=ALU.mult), [u_t[:], rstd[:]], [u_t[:]])

                def p1b(t):
                    x = x_t[t % 2]
                    hb = h_b[t % 2]
                    kb.op("DVE", lambda e: e.tensor_mul(out=sq[:], in0=u_t[:],
                                                        in1=u_t[:]),
                          [u_t[:]], [sq[:]])
                    mean2, rstd2 = ln_stats_pre(u_t, sq, pstat[1], 2)
                    kb.op("DVE", lambda e: e.tensor_tensor(
                        out=c3(hf[:]), in0=c3(u_t[:]), in1=bc(mean2[:]),
                        op=ALU.subtract), [u_t[:], mean2[:]], [hf[:]])
                    kb.op("DVE", lambda e: e.tensor_tensor(
                        out=c3(hf[:]), in0=c3(hf[:]), in1=bc(rstd2[:]),
                        op=ALU.mult), [hf[:], rstd2[:]], [hf[:]])
                    for c in range(NC):
                        cs = slice(c * T, (c + 1) * T)
                        kb.op("DVE", lambda e, c=c, cs=cs: e.tensor_scalar(
                            out=hb[:, cs], in0=hf[:, cs],
                            scalar1=lgam_c[:, c:c + 1], scalar2=lbet_c[:, c:c + 1],
                            op0=ALU.mult, op1=ALU.add),
                            [hf[:], lgam_c[:], lbet_c[:]], [hb[:]])
                    kb.op("DVE", lambda e: e.scalar_tensor_tensor(
                        out=x_t[t % 2][:], in0=u_t[:], scalar=0.5, in1=x[:],
                        op0=ALU.mult, op1=ALU.add), [u_t[:], x[:]], [x[:]])

                def p2y(t):
                    hb = h_b[t % 2]
                    for m in range(FC):
                        p = py[m % 2]
                        kb.mms([(p[:, 0:T],
                                 w1b[:, c * F + m * 128:c * F + (m + 1) * 128],
                                 hb[:, c * T:(c + 1) * T], c == 0, c == NC - 1)
                                for c in range(NC)])
                        kb.op("ACT", lambda e, p=p, m=m: e.activation(
                            out=y_b[:, m * T:(m + 1) * T], in_=p[:, 0:T],
                            func=AF.Silu, bias=b1_c[:, m:m + 1], scale=1.0),
                            [p[:], b1_c[:]], [y_b[:]])

                def p2z(t, ms):
                    x = x_t[t % 2]
                    for m in ms:
                        p = pz[m % 2]
                        kb.mms([(p[:, 0:T],
                                 w2b[:, f * D + m * 128:f * D + (m + 1) * 128],
                                 y_b[:, f * T:(f + 1) * T], f == 0, f == FC - 1)
                                for f in range(FC)])
                        kb.op("DVE", lambda e, p=p, m=m: e.scalar_tensor_tensor(
                            out=x[:, m * T:(m + 1) * T], in0=p[:, 0:T],
                            scalar=b2_c[:, m:m + 1], in1=x[:, m * T:(m + 1) * T],
                            op0=ALU.add, op1=ALU.add),
                            [p[:], b2_c[:], x[:]], [x[:]])

                load(0)
                p1a(0)
                p1b(0)
                for t in range(nT):
                    if t + 1 < nT:
                        load(t + 1)
                    p2y(t)
                    if t + 1 < nT:
                        p1a(t + 1)
                    p2z(t, range(0, 4))
                    if t + 1 < nT:
                        p1b(t + 1)
                    p2z(t, range(4, NC))
                    sts[t % 2].dma(dstv[:, :, t * T:(t + 1) * T], c3(x_t[t % 2][:]))
                for s in sts:
                    s.wait_all("SP")

        esW = contextlib.ExitStack()
        with esW:
            w1b = esW.enter_context(nc.sbuf_tensor("w1b", [128, NC * F], BF16))
            w2b = esW.enter_context(nc.sbuf_tensor("w2b", [128, FC * D], BF16))
            with contextlib.ExitStack() as esD:
                sqi = [esD.enter_context(nc.sbuf_tensor("dq_i%d" % i, [128, F], I32))
                       for i in (0, 1)]
                srow = [esD.enter_context(nc.sbuf_tensor("dq_r%d" % i, [1, F], F32))
                        for i in (0, 1)]
                psc = [esD.enter_context(nc.psum_tensor("dq_p%d" % i, [128, 512],
                                                        F32)) for i in range(4)]
                ldq = [kb.stream("ldq%d" % i) for i in (0, 1)]
                stw = kb.stream("stw")

                def dq_load(n, jobs):
                    qd, sd, c, width, *_ = jobs[n]
                    i = n % 2
                    ldq[i].dma(sqi[i][:, 0:width], qd[c * 128:(c + 1) * 128, :])
                    ldq[i].dma(srow[i][:, 0:width], sd[c:c + 1, :])

                def dq_proc(n, jobs):
                    qd, sd, c, width, wout, wdram, oc = jobs[n]
                    i = n % 2
                    ldq[i].wait_all("PE")
                    ldq[i].wait_all("DVE")
                    for j in range(width // 512):
                        p = psc[j % 4]
                        kb.mms([(p[:, 0:512], oc,
                                 srow[i][0:1, j * 512:(j + 1) * 512], True, True)])
                        kb.op("DVE", lambda e, p=p, j=j, i=i, c=c, width=width:
                              e.scalar_tensor_tensor(
                                  out=wout[:, c * width + j * 512:
                                           c * width + (j + 1) * 512],
                                  in0=sqi[i][:, j * 512:(j + 1) * 512],
                                  scalar=128.0, in1=p[:, 0:512],
                                  op0=ALU.subtract, op1=ALU.mult),
                              [sqi[i][:], p[:]], [wout[:]])
                    stw.dma(wdram[:, c * width:(c + 1) * width],
                            wout[:, c * width:(c + 1) * width])

                jobs = ([(fc1q, fc1s, c, F, w1b, w1d, ones_f[:]) for c in range(NC)]
                        + [(fc2q, fc2s, c, D, w2b, w2d, quart_f[:])
                           for c in range(FC)])
                dq_load(0, jobs)
                for n in range(len(jobs)):
                    if n + 1 < len(jobs):
                        dq_load(n + 1, jobs)
                    dq_proc(n, jobs)
                stw.wait_all("SP")
            ffn_phase(xT, x1d, w1b, w2b, "A")
        kb.barrier()

        esKT = contextlib.ExitStack()
        with esKT:
            kTs = esKT.enter_context(nc.sbuf_tensor("kTs", [128, NC * S], BF16))
            esB = contextlib.ExitStack()
            with esB:
                sbB = lambda n, sh, dt: esB.enter_context(nc.sbuf_tensor(n, sh, dt))
                psB = lambda n, sh: esB.enter_context(nc.psum_tensor(n, sh, F32))
                Wqb = sbB("Wqb", [128, NC * D], BF16)
                Wkb = sbB("Wkb", [128, NC * D], BF16)
                Wvb = sbB("Wvb", [128, NC * D], BF16)
                bvb = sbB("bvb", [128, D], F32)
                wtmp = [sbB("wtmp%d" % i, [128, D], F32) for i in (0, 1)]
                x_t = [sbB("bx_t%d" % i, [128, NC * T], F32) for i in (0, 1)]
                u2_t = sbB("bu2", [128, NC * T], F32)
                u2b = [sbB("bu2b%d" % i, [128, NC * T], BF16) for i in (0, 1)]
                xb = sbB("bxb", [128, NC * T], BF16)
                sq = sbB("bsq", [128, NC * T], BF16)
                q_b = sbB("bq_b", [128, NC * T], BF16)
                v_sb = sbB("bv_sb", [128, 1040], BF16)
                qsq = sbB("bqsq", [128, NC * T], BF16)
                n2m = sbB("bn2m", [16, 1], F32)
                pstat = [psB("bpstat%d" % i, [128, 512]) for i in (0, 1)]
                pp = [psB("bpp%d" % i, [128, 512]) for i in (0, 1)]
                pv = [psB("bpv%d" % i, [128, 512]) for i in (0, 1)]
                pqn = psB("bpqn", [16, 512])
                pkn = psB("bpkn", [16, 512])
                ldw = kb.stream("ldwB")
                lds = [kb.stream("ldb%d" % i) for i in (0, 1)]
                sts = [kb.stream("stb%d" % i) for i in (0, 1)]

                wjobs = [(Wd, Wsb, c) for Wd, Wsb in
                         ((Wq, Wqb), (Wk, Wkb), (Wv, Wvb)) for c in range(NC)]
                ldww = [kb.stream("ldww%d" % i) for i in (0, 1)]
                ldww[0].dma(wtmp[0][:], wjobs[0][0][0:128, :])
                for n, (Wd, Wsb, c) in enumerate(wjobs):
                    if n + 1 < len(wjobs):
                        W2, _, c2 = wjobs[n + 1]
                        ldww[(n + 1) % 2].dma(wtmp[(n + 1) % 2][:],
                                              W2[c2 * 128:(c2 + 1) * 128, :])
                    ldww[n % 2].wait_all("DVE")
                    w = wtmp[n % 2]
                    kb.op("DVE", lambda e, w=w, Wsb=Wsb, c=c: e.tensor_copy(
                        out=Wsb[:, c * D:(c + 1) * D], in_=w[:]),
                        [w[:]], [Wsb[:]])
                ldw.dma(bvb[:], bv.rearrange("(o d) -> o d", o=1)
                        .partition_broadcast(128))

                kb.op("DVE", lambda e: e.memset(v_sb[:], 1.0), [], [v_sb[:]])
                x1v, u2v, qv = fm(x1d), fm(u2d), fm(qTd)
                kv3 = kTs.ap().rearrange("p (c s) -> p c s", s=S)

                def loadB(t):
                    lds[t % 2].dma(c3(x_t[t % 2][:]), x1v[:, :, t * T:(t + 1) * T])

                def bp1(t):
                    x = x_t[t % 2]
                    ub = u2b[t % 2]
                    lds[t % 2].wait_all("POOL")
                    mean, rstd = ln_stats(x[:], xb, sq, pstat[0], 1)
                    kb.op("DVE", lambda e: e.tensor_tensor(
                        out=c3(u2_t[:]), in0=c3(x[:]), in1=bc(mean[:]),
                        op=ALU.subtract), [x[:], mean[:]], [u2_t[:]])
                    kb.op("DVE", lambda e, ub=ub: e.tensor_tensor(
                        out=c3(ub[:]), in0=c3(u2_t[:]), in1=bc(rstd[:]),
                        op=ALU.mult), [u2_t[:], rstd[:]], [ub[:]])
                    sts[t % 2].dma(u2v[:, :, t * T:(t + 1) * T], c3(ub[:]))

                def bp2q(t):
                    ub = u2b[t % 2]
                    for m in range(NC):
                        p = pp[m % 2]
                        kb.mms([(p[:, 0:T],
                                 Wqb[:, c * D + m * 128:c * D + (m + 1) * 128],
                                 ub[:, c * T:(c + 1) * T], c == 0, c == NC - 1)
                                for c in range(NC)])
                        kb.op("ACT", lambda e, p=p, m=m: e.activation(
                            out=q_b[:, m * T:(m + 1) * T], in_=p[:, 0:T],
                            func=AF.Identity, bias=bq_c[:, m:m + 1], scale=1.0),
                            [p[:], bq_c[:]], [q_b[:]])
                    sts[t % 2].dma(qv[:, :, t * T:(t + 1) * T], c3(q_b[:]))
                    kb.op("DVE", lambda e: e.tensor_mul(
                        out=qsq[:], in0=q_b[:], in1=q_b[:]), [q_b[:]], [qsq[:]])
                    kb.mms([(pqn[0:16, 0:T], sel_sb[:, c * 16:(c + 1) * 16],
                             qsq[:, c * T:(c + 1) * T], c == 0, c == NC - 1)
                            for c in range(NC)])
                    kb.op("DVE", lambda e: e.reduce_max(
                        out=n2m[:], in_=pqn[0:16, 0:T], axis=AX.X),
                        [pqn[:]], [n2m[:]])
                    if t == 0:
                        kb.op("DVE", lambda e: e.tensor_copy(
                            out=qn2run[:], in_=n2m[:]), [n2m[:]], [qn2run[:]])
                    else:
                        kb.op("DVE", lambda e: e.tensor_tensor(
                            out=qn2run[:], in0=qn2run[:], in1=n2m[:], op=ALU.max),
                            [qn2run[:], n2m[:]], [qn2run[:]])

                def bp2k(t):
                    ub = u2b[t % 2]
                    for m in range(NC):
                        p = pp[m % 2]
                        kb.mms([(p[:, 0:T],
                                 Wkb[:, c * D + m * 128:c * D + (m + 1) * 128],
                                 ub[:, c * T:(c + 1) * T], c == 0, c == NC - 1)
                                for c in range(NC)])
                        kb.op("ACT", lambda e, p=p, m=m, t=t: e.activation(
                            out=kTs[:, m * S + t * T:m * S + (t + 1) * T],
                            in_=p[:, 0:T], func=AF.Identity,
                            bias=bk_c[:, m:m + 1], scale=1.0),
                            [p[:], bk_c[:]], [kTs[:]])
                    kb.op("DVE", lambda e, t=t: e.tensor_mul(
                        out=c3(qsq[:]), in0=kv3[:, :, t * T:(t + 1) * T],
                        in1=kv3[:, :, t * T:(t + 1) * T]), [kTs[:]], [qsq[:]])
                    kb.mms([(pkn[0:16, 0:T], sel_sb[:, c * 16:(c + 1) * 16],
                             qsq[:, c * T:(c + 1) * T], c == 0, c == NC - 1)
                            for c in range(NC)])
                    kb.op("DVE", lambda e: e.reduce_max(
                        out=n2m[:], in_=pkn[0:16, 0:T], axis=AX.X),
                        [pkn[:]], [n2m[:]])
                    if t == 0:
                        kb.op("DVE", lambda e: e.tensor_copy(
                            out=kn2run[:], in_=n2m[:]), [n2m[:]], [kn2run[:]])
                    else:
                        kb.op("DVE", lambda e: e.tensor_tensor(
                            out=kn2run[:], in0=kn2run[:], in1=n2m[:], op=ALU.max),
                            [kn2run[:], n2m[:]], [kn2run[:]])

                def bp2v(t):
                    ub = u2b[t % 2]
                    for sub in range(T // 128):
                        for dvc in range(2):
                            p = pv[dvc]
                            kb.mms([(p[:, 0:512],
                                     ub[:, c * T + sub * 128:c * T + (sub + 1) * 128],
                                     Wvb[:, c * D + dvc * 512:c * D + (dvc + 1) * 512],
                                     c == 0, c == NC - 1) for c in range(NC)])
                            vout = v_sb.ap()[:, dvc * 520:dvc * 520 + 520] \
                                .rearrange("p (h c) -> p h c", c=65)[:, 0:8, 0:64]
                            kb.op("DVE", lambda e, p=p, dvc=dvc, vout=vout:
                                  e.tensor_tensor(
                                      out=vout,
                                      in0=p.ap()[:, 0:512].rearrange(
                                          "p (h c) -> p h c", c=64),
                                      in1=bvb.ap()[:, dvc * 512:(dvc + 1) * 512]
                                      .rearrange("p (h c) -> p h c", c=64),
                                      op=ALU.add),
                                  [p[:], bvb[:]], [v_sb[:]])
                        row0 = t * T + sub * 128
                        sts[t % 2].dma(vd[row0:row0 + 128, :], v_sb[:])

                loadB(0)
                bp1(0)
                for t in range(nT):
                    if t + 1 < nT:
                        loadB(t + 1)
                    bp2q(t)
                    if t + 1 < nT:
                        bp1(t + 1)
                    bp2k(t)
                    bp2v(t)
                kb.op("DVE", lambda e: e.tensor_mul(out=qn2run[:], in0=qn2run[:],
                                                    in1=kn2run[:]),
                      [qn2run[:], kn2run[:]], [qn2run[:]])
                kb.op("ACT", lambda e: e.activation(out=qn2run[:], in_=qn2run[:],
                                                    func=AF.Sqrt, bias=0.0,
                                                    scale=1.0 / 64.0),
                      [qn2run[:]], [qn2run[:]])
                kb.op("PE", lambda e: e.transpose(out=pstat[0][0:1, 0:16],
                                                  in_=qn2run[:],
                                                  identity=id16_sb[:]),
                      [qn2run[:], id16_sb[:]], [pstat[0][:]])
                kb.op("DVE", lambda e: e.tensor_scalar_mul(
                    out=negbrow[:], in0=pstat[0][0:1, 0:16], scalar1=-1.0),
                    [pstat[0][:]], [negbrow[:]])
                kb.op("DVE", lambda e: e.tensor_tensor(
                    out=negbrow[:, 0:8], in0=negbrow[0:1, 0:16:2],
                    in1=negbrow[0:1, 1:16:2], op=ALU.min),
                    [negbrow[:]], [negbrow[:]])
                kb.mms([(pstat[1][0:128, 0:8], ones_f[:], negbrow[:, 0:8],
                         True, True)])
                kb.op("DVE", lambda e: e.tensor_copy(out=negb_c[:],
                                                     in_=pstat[1][0:128, 0:8]),
                      [pstat[1][:]], [negb_c[:]])
                for s in sts:
                    s.wait_all("SP")
            kb.barrier()

            esT = contextlib.ExitStack()
            with esT:
                sbT = lambda n, sh, dt: esT.enter_context(nc.sbuf_tensor(n, sh, dt))
                psT = lambda n, sh: esT.enter_context(nc.psum_tensor(n, sh, F32))
                v65 = sbT("v65", [128, nKB * 1040], BF16)
                qTg = [sbT("qTg%d" % i, [128, NC * QG], BF16) for i in (0, 1)]
                PT = [sbT("PT%d" % i, [128, 1024], BF16) for i in (0, 1)]
                oTg = [sbT("oTg%d" % i, [128, NC * QG], BF16) for i in (0, 1)]
                ssb = [sbT("ssb%d" % i, [65, 1024], F32) for i in (0, 1)]
                pS = [psT("pS%d" % i, [128, 1024]) for i in (0, 1)]
                pAV = [psT("pAV%d" % i, [128, 512]) for i in range(4)]
                ldv = kb.stream("ldv")
                ldqg = [kb.stream("ldqg%d" % i) for i in (0, 1)]
                sto = kb.stream("sto")

                vsrc = vd.rearrange("(b p) c -> p b c", p=128)
                vdst = v65.ap().rearrange("p (b c) -> p b c", c=1040)
                for j in range(4):
                    blk = nKB // 4
                    ldv.dma(vdst[:, j * blk:(j + 1) * blk, :],
                            vsrc[:, j * blk:(j + 1) * blk, :])
                qgv = fm(qTd)

                def loadG(g):
                    ldqg[g % 2].dma(
                        qTg[g % 2].ap().rearrange("p (c s) -> p c s", s=QG),
                        qgv[:, :, g * QG:(g + 1) * QG])

                loadG(0)
                for g in range(nG):
                    if g + 1 < nG:
                        loadG(g + 1)
                    qg = qTg[g % 2]
                    ldqg[g % 2].wait_all("PE")
                    ldv.wait_all("PE")
                    kbmax = 4 * g + 3
                    units = [(c, kbi) for c in range(8)
                             for kbi in range(kbmax + 1)]

                    def emit_S(idx):
                        c, kbi = units[idx]
                        p = pS[idx % 2]
                        lA = kTs[0:64, c * S + kbi * 128:c * S + (kbi + 1) * 128]
                        lB = kTs[64:128, c * S + kbi * 128:c * S + (kbi + 1) * 128]
                        rA = qg[0:64, c * QG:(c + 1) * QG]
                        rB = qg[64:128, c * QG:(c + 1) * QG]
                        m = kbi - 4 * g
                        items = []
                        for off, l, r in ((0, lA, rA), (512, lB, rB)):
                            if 0 <= m:
                                items += [(p[:, off:off + 512], l, r,
                                           True, False),
                                          (p[:, off + m * 128:
                                             off + (m + 1) * 128],
                                           id128_sb[:], mask_sb[:],
                                           False, True)]
                            else:
                                items.append((p[:, off:off + 512], l, r,
                                              True, True))
                        kb.mms(items)

                    def emit_exp(idx):
                        c, kbi = units[idx]
                        p = pS[idx % 2]
                        pt = PT[idx % 2]
                        kb.op("ACT", lambda e, p=p, pt=pt, c=c: e.activation(
                            out=pt[:], in_=p[:], func=AF.Exp,
                            bias=negb_c[:, c:c + 1], scale=0.125),
                            [p[:], negb_c[:]], [pt[:]])

                    def emit_AV(idx):
                        c, kbi = units[idx]
                        pt = PT[idx % 2]
                        pavA, pavB = pAV[2 * (c % 2)], pAV[2 * (c % 2) + 1]
                        hA, hB = 2 * c, 2 * c + 1
                        m = max(kbi - 4 * g, 0)
                        q0 = m * 128
                        vA = v65[:, kbi * 1040 + hA * 65:
                                 kbi * 1040 + hA * 65 + 65]
                        vB = v65[:, kbi * 1040 + hB * 65:
                                 kbi * 1040 + hB * 65 + 65]
                        kb.mms([(pavA[0:65, q0:QG], vA, pt[:, q0:512],
                                 kbi == 0, kbi == kbmax),
                                (pavB[0:65, q0:QG], vB, pt[:, 512 + q0:1024],
                                 kbi == 0, kbi == kbmax)])

                    def emit_tail(c):
                        og = oTg[g % 2]
                        sb_ = ssb[c % 2]
                        pavA, pavB = pAV[2 * (c % 2)], pAV[2 * (c % 2) + 1]
                        for hh, pav in ((0, pavA), (1, pavB)):
                            rows = slice(0, 64) if hh == 0 else slice(64, 128)
                            kb.op("DVE", lambda e, pav=pav, rows=rows, c=c,
                                  og=og: e.tensor_copy(
                                      out=og[rows, c * QG:(c + 1) * QG],
                                      in_=pav[0:64, 0:QG]),
                                  [pav[:]], [og[:]])
                            kb.op("DVE", lambda e, pav=pav, sb_=sb_, hh=hh:
                                  e.tensor_copy(
                                      out=sb_[64:65, hh * QG:(hh + 1) * QG],
                                      in_=pav[64:65, 0:QG]),
                                  [pav[:]], [sb_[:]])
                            sto.dma(sumd[2 * c + hh:2 * c + hh + 1,
                                         g * QG:(g + 1) * QG],
                                    sb_[64:65, hh * QG:(hh + 1) * QG])

                    emit_S(0)
                    for idx in range(len(units)):
                        emit_exp(idx)
                        if idx + 1 < len(units):
                            emit_S(idx + 1)
                        emit_AV(idx)
                        c, kbi = units[idx]
                        if kbi == kbmax:
                            emit_tail(c)
                    sto.dma(fm(oTd)[:, :, g * QG:(g + 1) * QG],
                            oTg[g % 2].ap().rearrange("p (c s) -> p c s", s=QG))
                sto.wait_all("SP")
        kb.barrier()

        esE = contextlib.ExitStack()
        with esE:
            sbE = lambda n, sh, dt: esE.enter_context(nc.sbuf_tensor(n, sh, dt))
            psE = lambda n, sh: esE.enter_context(nc.psum_tensor(n, sh, F32))
            Wob = sbE("Wob", [128, NC * D], BF16)
            wtmp = [sbE("ewtmp%d" % i, [128, D], F32) for i in (0, 1)]
            o_t = [sbE("eo_t%d" % i, [128, NC * T], BF16) for i in (0, 1)]
            u2_t = [sbE("eu2%d" % i, [128, NC * T], BF16) for i in (0, 1)]
            x1_t = [sbE("ex1%d" % i, [128, NC * T], F32) for i in (0, 1)]
            y_t = [sbE("ey_t%d" % i, [128, NC * T], F32) for i in (0, 1)]
            xb = sbE("exb", [128, NC * T], BF16)
            sq = sbE("esq", [128, NC * T], BF16)
            ssb2 = sbE("essum", [16, S], F32)
            rr_b = sbE("err_b", [16, S], BF16)
            pstat = [psE("epstat%d" % i, [128, 512]) for i in (0, 1)]
            po = [psE("epo%d" % i, [128, 512]) for i in (0, 1)]
            pR = [psE("epR%d" % i, [128, 512]) for i in (0, 1)]
            ldw = kb.stream("ldwE")
            lds = [kb.stream("lde%d" % i) for i in (0, 1)]
            sts = [kb.stream("ste%d" % i) for i in (0, 1)]

            ldsm = kb.stream("ldsm")
            ldsm.dma(ssb2[:], sumd)
            ldsm.wait_all("DVE")
            kb.op("DVE", lambda e: e.reciprocal(out=ssb2[:], in_=ssb2[:]),
                  [ssb2[:]], [ssb2[:]])
            kb.op("DVE", lambda e: e.tensor_copy(out=rr_b[:], in_=ssb2[:]),
                  [ssb2[:]], [rr_b[:]])
            ldwe = [kb.stream("ldwe%d" % i) for i in (0, 1)]
            ldwe[0].dma(wtmp[0][:], Wo[0:128, :])
            for c in range(NC):
                if c + 1 < NC:
                    ldwe[(c + 1) % 2].dma(wtmp[(c + 1) % 2][:],
                                          Wo[(c + 1) * 128:(c + 2) * 128, :])
                ldwe[c % 2].wait_all("DVE")
                w = wtmp[c % 2]
                kb.op("DVE", lambda e, w=w, c=c: e.tensor_copy(
                    out=Wob[:, c * D:(c + 1) * D], in_=w[:]), [w[:]], [Wob[:]])
            ov, u2v, x1v, x2v = fm(oTd), fm(u2d), fm(x1d), fm(x2d)

            def loadE(t):
                i = t % 2
                lds[i].dma(c3(o_t[i][:]), ov[:, :, t * T:(t + 1) * T])
                lds[i].dma(c3(u2_t[i][:]), u2v[:, :, t * T:(t + 1) * T])
                lds[i].dma(c3(x1_t[i][:]), x1v[:, :, t * T:(t + 1) * T])


            def ep1(t):
                i = t % 2
                lds[i].wait_all("PE")
                lds[i].wait_all("DVE")
                for m in range(NC):
                    pr = pR[m % 2]
                    kb.mms([(pr[:, 0:T], selh_sb[:, m * 128:(m + 1) * 128],
                             rr_b[0:16, t * T:(t + 1) * T], True, True)])
                    kb.op("DVE", lambda e, i=i, m=m, pr=pr: e.tensor_mul(
                        out=o_t[i][:, m * T:(m + 1) * T],
                        in0=o_t[i][:, m * T:(m + 1) * T], in1=pr[:, 0:T]),
                        [o_t[i][:], pr[:]], [o_t[i][:]])
                for m in range(NC):
                    p = po[m % 2]
                    kb.mms([(p[:, 0:T],
                             Wob[:, c * D + m * 128:c * D + (m + 1) * 128],
                             o_t[i][:, c * T:(c + 1) * T], c == 0, c == NC - 1)
                            for c in range(NC)])
                    kb.op("DVE", lambda e, p=p, m=m, i=i: e.scalar_tensor_tensor(
                        out=y_t[i][:, m * T:(m + 1) * T], in0=p[:, 0:T],
                        scalar=bo_c[:, m:m + 1], in1=u2_t[i][:, m * T:(m + 1) * T],
                        op0=ALU.add, op1=ALU.add),
                        [p[:], bo_c[:], u2_t[i][:]], [y_t[i][:]])

            def ep2(t):
                i = t % 2
                y, x1 = y_t[i], x1_t[i]
                mean, rstd = ln_stats(y[:], xb, sq, pstat[0], 1)
                kb.op("POOL", lambda e, y=y: e.tensor_tensor(
                    out=c3(y[:]), in0=c3(y[:]), in1=bc(mean[:]),
                    op=ALU.subtract), [y[:], mean[:]], [y[:]])
                kb.op("POOL", lambda e, y=y: e.tensor_tensor(
                    out=c3(y[:]), in0=c3(y[:]), in1=bc(rstd[:]),
                    op=ALU.mult), [y[:], rstd[:]], [y[:]])
                for m in range(NC):
                    ms = slice(m * T, (m + 1) * T)
                    kb.op("DVE", lambda e, m=m, ms=ms, x1=x1, y=y:
                          e.scalar_tensor_tensor(
                              out=x1[:, ms], in0=y[:, ms],
                              scalar=agam_c[:, m:m + 1], in1=x1[:, ms],
                              op0=ALU.mult, op1=ALU.add),
                          [y[:], agam_c[:], x1[:]], [x1[:]])
                    kb.op("DVE", lambda e, m=m, ms=ms, x1=x1: e.tensor_scalar(
                        out=x1[:, ms], in0=x1[:, ms], scalar1=abet_c[:, m:m + 1],
                        scalar2=None, op0=ALU.add),
                        [x1[:], abet_c[:]], [x1[:]])
                sts[i].dma(x2v[:, :, t * T:(t + 1) * T], c3(x1[:]))

            loadE(0)
            ep1(0)
            for t in range(nT):
                if t + 1 < nT:
                    loadE(t + 1)
                ep2(t)
                if t + 1 < nT:
                    ep1(t + 1)
            for s in sts:
                s.wait_all("SP")
        kb.barrier()

        esC = contextlib.ExitStack()
        with esC:
            w1b = esC.enter_context(nc.sbuf_tensor("w1bC", [128, NC * F], BF16))
            w2b = esC.enter_context(nc.sbuf_tensor("w2bC", [128, FC * D], BF16))
            ldw = kb.stream("ldwC")
            for j in range(4):
                w = NC * F // 4
                ldw.dma(w1b[:, j * w:(j + 1) * w], w1d[:, j * w:(j + 1) * w])
                ldw.dma(w2b[:, j * w:(j + 1) * w], w2d[:, j * w:(j + 1) * w])
            ldw.wait_all("PE")
            ffn_phase(x2d, outT, w1b, w2b, "C")
    return nc


def kernel(**inputs):
    S = 4096
    x = np.asarray(inputs["L0_attn_x"], dtype=np.float32)
    nm = {
        "Wq": "L0_attn_Wq", "bq": "L0_attn_bq", "Wk": "L0_attn_Wk",
        "bk": "L0_attn_bk", "Wv": "L0_attn_Wv", "bv": "L0_attn_bv",
        "Wo": "L0_attn_Wo", "bo": "L0_attn_bo", "agam": "L0_attn_gamma",
        "abet": "L0_attn_beta", "lgam": "L0_ffn_ln_gamma",
        "lbet": "L0_ffn_ln_beta", "fc1s": "L0_ffn_fc1_scales",
        "fc1b": "L0_ffn_fc1_bias", "fc2s": "L0_ffn_fc2_scales",
        "fc2b": "L0_ffn_fc2_bias",
    }
    base = {k: np.ascontiguousarray(np.asarray(inputs[v], dtype=np.float32))
            for k, v in nm.items()}
    base["fc1q"] = np.ascontiguousarray(np.asarray(inputs["L0_ffn_fc1_q"],
                                                   dtype=np.int32))
    base["fc2q"] = np.ascontiguousarray(np.asarray(inputs["L0_ffn_fc2_q"],
                                                   dtype=np.int32))
    in_maps = []
    for b in range(8):
        m = dict(base)
        m["xT"] = np.ascontiguousarray(x[b].T)
        in_maps.append(m)
    nc = build_nc(S)
    res = run_bass_kernel_spmd(nc, in_maps, core_ids=list(range(8)))
    out = np.stack([np.asarray(r["outT"]).T for r in res.results], axis=0)
    return out.astype(np.float32)


# version 12
# speedup vs baseline: 1.0141x; 1.0141x over previous
import contextlib
import numpy as np

try:
    import concourse.bass as bass
except ImportError:
    import sys
    sys.path.insert(0, "/opt/trn_rl_repo")
    import concourse.bass as bass
import concourse.mybir as mybir
from concourse.bass_utils import run_bass_kernel_spmd

F32 = mybir.dt.float32
BF16 = mybir.dt.bfloat16
I32 = mybir.dt.int32
AF = mybir.ActivationFunctionType
ALU = mybir.AluOpType
AX = mybir.AxisListType

D = 1024
F = 4096
H = 16
HD = 64
NC = 8
FC = 32
T = 256
QG = 512
EPS = 1e-5
NEG = -1.0e30


class KB:

    def __init__(self, nc, es):
        self.nc = nc
        self.es = es
        self.eng = dict(PE=nc.tensor, ACT=nc.scalar, DVE=nc.vector,
                        POOL=nc.gpsimd, SP=nc.sync)
        self.sem = {k: es.enter_context(nc.semaphore("e_" + k)) for k in self.eng}
        self.cnt = {k: 0 for k in self.eng}
        self.wr = {}
        self.rd = {}
        self.waited = {}
        self.semobj = {k: self.sem[k] for k in self.eng}
        self.streams = {}

    def _wait(self, eng, semkey, val):
        if semkey == eng:
            return
        if semkey in self.streams:
            val = 16 * self.streams[semkey].issued
        else:
            assert val is not None and val <= self.cnt[semkey], (semkey, val)
        w = (eng, semkey)
        if self.waited.get(w, -1) >= val:
            return
        self.eng[eng].wait_ge(self.semobj[semkey], val)
        self.waited[w] = val

    def dep(self, eng, rkeys, wkeys):
        for k in rkeys:
            if k in self.wr:
                s, v = self.wr[k]
                self._wait(eng, s, v)
        for k in wkeys:
            if k in self.wr:
                s, v = self.wr[k]
                self._wait(eng, s, v)
            for s, v in self.rd.get(k, {}).items():
                self._wait(eng, s, v)

    def _note(self, eng_semkey, val, rkeys, wkeys):
        for k in wkeys:
            self.wr[k] = (eng_semkey, val)
            self.rd[k] = {}
        for k in rkeys:
            self.rd.setdefault(k, {})[eng_semkey] = val

    @staticmethod
    def keys(aps):
        return [a.name for a in aps]

    def op(self, eng, fn, reads, writes):
        rk, wk = self.keys(reads), self.keys(writes)
        self.dep(eng, rk, wk)
        ins = fn(self.eng[eng])
        self.cnt[eng] += 1
        ins.then_inc(self.sem[eng], 1)
        self._note(eng, self.cnt[eng], rk, wk)
        return ins

    def mms(self, items):
        rk, wk = [], []
        for o, l, r, st, sp in items:
            rk += self.keys([l, r])
            wk.append(o.name)
        self.dep("PE", rk, wk)
        for o, l, r, st, sp in items:
            ins = self.eng["PE"].matmul(out=o, lhsT=l, rhs=r, start=st, stop=sp)
        self.cnt["PE"] += 1
        ins.then_inc(self.sem["PE"], 1)
        self._note("PE", self.cnt["PE"], rk, wk)

    def barrier(self):
        self.nc.all_engine_barrier()
        self.wr = {k: v for k, v in self.wr.items() if v[0] in self.streams}
        self.rd = {k: {s: v for s, v in d.items() if s in self.streams}
                   for k, d in self.rd.items()}
        self.rd = {k: d for k, d in self.rd.items() if d}
        self.waited = {}

    def stream(self, name):
        s = DmaStream(self, name, self.es)
        self.streams[name] = s
        self.semobj[name] = s.sem
        return s


class DmaStream:
    def __init__(self, kb, name, es):
        self.kb = kb
        self.name = name
        self.sem = es.enter_context(kb.nc.semaphore("dma_" + name))
        self.issued = 0

    def dma(self, out, in_, **kw):
        kb = self.kb
        rk, wk = [in_.name], [out.name]
        kb.dep("SP", rk, wk)
        ins = kb.eng["SP"].dma_start(out=out, in_=in_, **kw)
        ins.then_inc(self.sem, 16)
        self.issued += 1
        kb._note(self.name, None, rk, wk)

    def wait_all(self, eng):
        self.kb._wait(eng, self.name, None)


def build_nc(S):
    nT = S // T
    nG = S // QG
    nKB = S // 128
    nc = bass.Bass("TRN2", target_bir_lowering=False,
                   detect_race_conditions=False)

    dt_in = lambda n, sh, dt: nc.dram_tensor(n, sh, dt, kind="ExternalInput").ap()
    dt_int = lambda n, sh, dt: nc.dram_tensor(n, sh, dt, kind="Internal").ap()

    xT = dt_in("xT", [D, S], F32)
    Wq, Wk, Wv, Wo = (dt_in(n, [D, D], F32) for n in ("Wq", "Wk", "Wv", "Wo"))
    bq, bk, bv, bo = (dt_in(n, [D], F32) for n in ("bq", "bk", "bv", "bo"))
    agam, abet = dt_in("agam", [D], F32), dt_in("abet", [D], F32)
    lgam, lbet = dt_in("lgam", [D], F32), dt_in("lbet", [D], F32)
    fc1q = dt_in("fc1q", [D, F], I32)
    fc1s = dt_in("fc1s", [NC, F], F32)
    fc1b = dt_in("fc1b", [F], F32)
    fc2q = dt_in("fc2q", [F, D], I32)
    fc2s = dt_in("fc2s", [FC, D], F32)
    fc2b = dt_in("fc2b", [D], F32)
    outT = nc.dram_tensor("outT", [D, S], F32, kind="ExternalOutput").ap()

    x1d = dt_int("x1d", [D, S], F32)
    u2d = dt_int("u2d", [D, S], BF16)
    x2d = dt_int("x2d", [D, S], F32)
    qTd = dt_int("qTd", [D, S], BF16)
    vd = dt_int("vd", [S, 1040], BF16)
    oTd = dt_int("oTd", [D, S], BF16)
    sumd = dt_int("sumd", [H, S], F32)
    w1d = dt_int("w1d", [128, NC * F], BF16)
    w2d = dt_int("w2d", [128, FC * D], BF16)

    import ml_dtypes
    mask_np = np.where(np.arange(128)[:, None] > np.arange(128)[None, :],
                       np.float32(NEG), np.float32(0.0))
    maskc = nc.inline_tensor(mask_np.astype(ml_dtypes.bfloat16), name="maskc").ap()
    id128c = nc.inline_tensor(np.eye(128, dtype=np.float32)
                              .astype(ml_dtypes.bfloat16), name="id128c").ap()
    sel_np = np.zeros((128, NC * 16), dtype=np.float32)
    for c in range(NC):
        sel_np[0:64, c * 16 + 2 * c] = 1.0
        sel_np[64:128, c * 16 + 2 * c + 1] = 1.0
    selc = nc.inline_tensor(sel_np.astype(ml_dtypes.bfloat16), name="selc").ap()
    selh_np = np.zeros((16, NC * 128), dtype=np.float32)
    for c in range(NC):
        selh_np[2 * c, c * 128:c * 128 + 64] = 1.0
        selh_np[2 * c + 1, c * 128 + 64:(c + 1) * 128] = 1.0
    selhc = nc.inline_tensor(selh_np.astype(ml_dtypes.bfloat16), name="selhc").ap()
    id16 = nc.inline_tensor(np.eye(16, dtype=np.float32), name="id16").ap()

    def fm(dram):
        return dram.rearrange("(c p) s -> p c s", p=128)

    es0 = contextlib.ExitStack()
    with es0:
        kb = KB(nc, es0)
        sb = lambda n, sh, dt: es0.enter_context(nc.sbuf_tensor(n, sh, dt))

        mask_sb = sb("mask_sb", [128, 128], BF16)
        id128_sb = sb("id128_sb", [128, 128], BF16)
        sel_sb = sb("sel_sb", [128, NC * 16], BF16)
        id16_sb = sb("id16_sb", [16, 16], F32)
        selh_sb = sb("selh_sb", [16, NC * 128], BF16)
        lgam_c = sb("lgam_c", [128, NC], F32)
        lbet_c = sb("lbet_c", [128, NC], F32)
        agam_c = sb("agam_c", [128, NC], F32)
        abet_c = sb("abet_c", [128, NC], F32)
        bq_c = sb("bq_c", [128, NC], F32)
        bk_c = sb("bk_c", [128, NC], F32)
        bo_c = sb("bo_c", [128, NC], F32)
        b1_c = sb("b1_c", [128, FC], F32)
        b2_c = sb("b2_c", [128, NC], F32)
        onesP = sb("onesP", [128, 128], BF16)
        eps_c = sb("eps_c", [128, 1], F32)
        ones_f = sb("ones_f", [1, 128], F32)
        quart_f = sb("quart_f", [1, 128], F32)
        ones_b = sb("ones_b", [1, 128], BF16)
        qn2run = sb("qn2run", [16, 1], F32)
        kn2run = sb("kn2run", [16, 1], F32)
        negbrow = sb("negbrow", [1, 16], F32)
        negb_c = sb("negb_c", [128, 8], F32)
        stat = {n: sb("st_" + n, [128, T], F32)
                for n in ("mean", "v", "r", "mean2", "v2", "r2")}

        ld0 = kb.stream("ld0")
        colv = lambda v: v.rearrange("(m p) -> p m", p=128)
        for dst, src in ((lgam_c, lgam), (lbet_c, lbet), (agam_c, agam),
                         (abet_c, abet), (bq_c, bq), (bk_c, bk), (bo_c, bo),
                         (b1_c, fc1b), (b2_c, fc2b)):
            ld0.dma(dst[:], colv(src), allow_slow_non_contiguous=True)
        ld0.dma(mask_sb[:], maskc)
        ld0.dma(id128_sb[:], id128c)
        ld0.dma(sel_sb[:], selc)
        ld0.dma(selh_sb[:], selhc)
        ld0.dma(id16_sb[:], id16)
        kb.op("DVE", lambda e: e.memset(onesP[:], 1.0 / 1024.0), [], [onesP[:]])
        kb.op("DVE", lambda e: e.memset(eps_c[:], EPS), [], [eps_c[:]])
        kb.op("DVE", lambda e: e.memset(ones_f[:], 1.0), [], [ones_f[:]])
        kb.op("DVE", lambda e: e.memset(quart_f[:], 0.25), [], [quart_f[:]])
        kb.op("DVE", lambda e: e.memset(ones_b[:], 1.0), [], [ones_b[:]])
        ld0.wait_all("DVE")
        kb.op("DVE", lambda e: e.tensor_scalar_mul(out=b2_c[:], in0=b2_c[:],
                                                   scalar1=0.25),
              [b2_c[:]], [b2_c[:]])

        def ln_stats(src, xb, sq, pstat, sfx):
            mean, var, rstd = (stat[k + ("" if sfx == 1 else "2")]
                               for k in ("mean", "v", "r"))
            kb.op("DVE", lambda e: e.tensor_copy(out=xb[:], in_=src[:]),
                  [src[:]], [xb[:]])
            kb.op("DVE", lambda e: e.tensor_mul(out=sq[:], in0=xb[:], in1=xb[:]),
                  [xb[:]], [sq[:]])
            kb.mms([(pstat[:, 0:T], onesP[:], xb[:, c * T:(c + 1) * T],
                     c == 0, c == NC - 1) for c in range(NC)])
            kb.mms([(pstat[:, T:2 * T], onesP[:], sq[:, c * T:(c + 1) * T],
                     c == 0, c == NC - 1) for c in range(NC)])
            kb.op("DVE", lambda e: e.tensor_copy(out=mean[:], in_=pstat[:, 0:T]),
                  [pstat[:]], [mean[:]])
            kb.op("DVE", lambda e: e.tensor_mul(out=var[:], in0=mean[:], in1=mean[:]),
                  [mean[:]], [var[:]])
            kb.op("DVE", lambda e: e.tensor_tensor(out=var[:], in0=pstat[:, T:2 * T],
                                                   in1=var[:], op=ALU.subtract),
                  [pstat[:], var[:]], [var[:]])
            kb.op("ACT", lambda e: e.activation(out=rstd[:], in_=var[:], func=AF.Sqrt,
                                                bias=eps_c[:], scale=1.0),
                  [var[:], eps_c[:]], [rstd[:]])
            kb.op("DVE", lambda e: e.reciprocal(out=rstd[:], in_=rstd[:]),
                  [rstd[:]], [rstd[:]])
            return mean, rstd

        def ln_stats_pre(xb_t, sq, pstat, sfx):
            mean, var, rstd = (stat[k + ("" if sfx == 1 else "2")]
                               for k in ("mean", "v", "r"))
            kb.mms([(pstat[:, 0:T], onesP[:], xb_t[:, c * T:(c + 1) * T],
                     c == 0, c == NC - 1) for c in range(NC)])
            kb.mms([(pstat[:, T:2 * T], onesP[:], sq[:, c * T:(c + 1) * T],
                     c == 0, c == NC - 1) for c in range(NC)])
            kb.op("DVE", lambda e: e.tensor_copy(out=mean[:], in_=pstat[:, 0:T]),
                  [pstat[:]], [mean[:]])
            kb.op("DVE", lambda e: e.tensor_mul(out=var[:], in0=mean[:], in1=mean[:]),
                  [mean[:]], [var[:]])
            kb.op("DVE", lambda e: e.tensor_tensor(out=var[:], in0=pstat[:, T:2 * T],
                                                   in1=var[:], op=ALU.subtract),
                  [pstat[:], var[:]], [var[:]])
            kb.op("ACT", lambda e: e.activation(out=rstd[:], in_=var[:], func=AF.Sqrt,
                                                bias=eps_c[:], scale=1.0),
                  [var[:], eps_c[:]], [rstd[:]])
            kb.op("DVE", lambda e: e.reciprocal(out=rstd[:], in_=rstd[:]),
                  [rstd[:]], [rstd[:]])
            return mean, rstd

        def bc(ap):
            return ap.rearrange("p (o s) -> p o s", o=1).broadcast_to([128, NC, T])

        def c3(ap):
            return ap.rearrange("p (c s) -> p c s", s=T)

        def ffn_phase(src_dram, dst_dram, w1b, w2b, ph):
            esA = contextlib.ExitStack()
            with esA:
                sbA = lambda n, sh, dt: esA.enter_context(
                    nc.sbuf_tensor(n + ph, sh, dt))
                psA = lambda n, sh: esA.enter_context(
                    nc.psum_tensor(n + ph, sh, F32))
                x_t = [sbA("x_t%d" % i, [128, NC * T], F32) for i in (0, 1)]
                u_t = sbA("u_t", [128, NC * T], BF16)
                hf = sbA("hf", [128, NC * T], F32)
                h_b = [sbA("h_b%d" % i, [128, NC * T], BF16) for i in (0, 1)]
                xb = sbA("xb", [128, NC * T], BF16)
                sq = sbA("sq", [128, NC * T], BF16)
                y_b = sbA("y_b", [128, FC * T], BF16)
                pstat = [psA("pstat%d" % i, [128, 512]) for i in (0, 1)]
                py = [psA("py%d" % i, [128, 512]) for i in (0, 1)]
                pz = [psA("pz%d" % i, [128, 512]) for i in (0, 1)]
                lds = [kb.stream("ldx%d" % i + ph) for i in (0, 1)]
                sts = [kb.stream("stx%d" % i + ph) for i in (0, 1)]
                srcv, dstv = fm(src_dram), fm(dst_dram)

                def load(t):
                    lds[t % 2].dma(c3(x_t[t % 2][:]), srcv[:, :, t * T:(t + 1) * T])

                def p1a(t):
                    x = x_t[t % 2]
                    lds[t % 2].wait_all("POOL")
                    mean, rstd = ln_stats(x[:], xb, sq, pstat[0], 1)
                    kb.op("DVE", lambda e: e.tensor_tensor(
                        out=c3(u_t[:]), in0=c3(x[:]), in1=bc(mean[:]),
                        op=ALU.subtract), [x[:], mean[:]], [u_t[:]])
                    kb.op("DVE", lambda e: e.tensor_tensor(
                        out=c3(u_t[:]), in0=c3(u_t[:]), in1=bc(rstd[:]),
                        op=ALU.mult), [u_t[:], rstd[:]], [u_t[:]])

                def p1b(t):
                    x = x_t[t % 2]
                    hb = h_b[t % 2]
                    kb.op("DVE", lambda e: e.tensor_mul(out=sq[:], in0=u_t[:],
                                                        in1=u_t[:]),
                          [u_t[:]], [sq[:]])
                    mean2, rstd2 = ln_stats_pre(u_t, sq, pstat[1], 2)
                    kb.op("DVE", lambda e: e.tensor_tensor(
                        out=c3(hf[:]), in0=c3(u_t[:]), in1=bc(mean2[:]),
                        op=ALU.subtract), [u_t[:], mean2[:]], [hf[:]])
                    kb.op("DVE", lambda e: e.tensor_tensor(
                        out=c3(hf[:]), in0=c3(hf[:]), in1=bc(rstd2[:]),
                        op=ALU.mult), [hf[:], rstd2[:]], [hf[:]])
                    for c in range(NC):
                        cs = slice(c * T, (c + 1) * T)
                        kb.op("DVE", lambda e, c=c, cs=cs: e.tensor_scalar(
                            out=hb[:, cs], in0=hf[:, cs],
                            scalar1=lgam_c[:, c:c + 1], scalar2=lbet_c[:, c:c + 1],
                            op0=ALU.mult, op1=ALU.add),
                            [hf[:], lgam_c[:], lbet_c[:]], [hb[:]])
                    kb.op("DVE", lambda e: e.scalar_tensor_tensor(
                        out=x_t[t % 2][:], in0=u_t[:], scalar=0.5, in1=x[:],
                        op0=ALU.mult, op1=ALU.add), [u_t[:], x[:]], [x[:]])

                def p2y(t):
                    hb = h_b[t % 2]
                    for m in range(FC):
                        p = py[m % 2]
                        kb.mms([(p[:, 0:T],
                                 w1b[:, c * F + m * 128:c * F + (m + 1) * 128],
                                 hb[:, c * T:(c + 1) * T], c == 0, c == NC - 1)
                                for c in range(NC)])
                        kb.op("ACT", lambda e, p=p, m=m: e.activation(
                            out=y_b[:, m * T:(m + 1) * T], in_=p[:, 0:T],
                            func=AF.Silu, bias=b1_c[:, m:m + 1], scale=1.0),
                            [p[:], b1_c[:]], [y_b[:]])

                def p2z(t, ms):
                    x = x_t[t % 2]
                    for m in ms:
                        p = pz[m % 2]
                        kb.mms([(p[:, 0:T],
                                 w2b[:, f * D + m * 128:f * D + (m + 1) * 128],
                                 y_b[:, f * T:(f + 1) * T], f == 0, f == FC - 1)
                                for f in range(FC)])
                        kb.op("DVE", lambda e, p=p, m=m: e.scalar_tensor_tensor(
                            out=x[:, m * T:(m + 1) * T], in0=p[:, 0:T],
                            scalar=b2_c[:, m:m + 1], in1=x[:, m * T:(m + 1) * T],
                            op0=ALU.add, op1=ALU.add),
                            [p[:], b2_c[:], x[:]], [x[:]])

                load(0)
                p1a(0)
                p1b(0)
                for t in range(nT):
                    if t + 1 < nT:
                        load(t + 1)
                    p2y(t)
                    if t + 1 < nT:
                        p1a(t + 1)
                    p2z(t, range(0, 4))
                    if t + 1 < nT:
                        p1b(t + 1)
                    p2z(t, range(4, NC))
                    sts[t % 2].dma(dstv[:, :, t * T:(t + 1) * T], c3(x_t[t % 2][:]))
                for s in sts:
                    s.wait_all("SP")

        esW = contextlib.ExitStack()
        with esW:
            w1b = esW.enter_context(nc.sbuf_tensor("w1b", [128, NC * F], BF16))
            w2b = esW.enter_context(nc.sbuf_tensor("w2b", [128, FC * D], BF16))
            with contextlib.ExitStack() as esD:
                sqi = [esD.enter_context(nc.sbuf_tensor("dq_i%d" % i, [128, F], I32))
                       for i in (0, 1)]
                scb = [esD.enter_context(nc.sbuf_tensor("dq_s%d" % i, [128, F], F32))
                       for i in (0, 1)]
                ldq = [kb.stream("ldq%d" % i) for i in (0, 1)]
                stw = kb.stream("stw")

                def dq_load(n, jobs):
                    qd, sd, c, width, *_ = jobs[n]
                    i = n % 2
                    ldq[i].dma(sqi[i][:, 0:width], qd[c * 128:(c + 1) * 128, :])
                    ldq[i].dma(scb[i][:, 0:width],
                               sd[c:c + 1, :].partition_broadcast(128))

                def dq_proc(n, jobs):
                    qd, sd, c, width, wout, wdram, quarter = jobs[n]
                    i = n % 2
                    ldq[i].wait_all("DVE")
                    if quarter:
                        kb.op("DVE", lambda e, i=i, width=width:
                              e.tensor_scalar_mul(
                                  out=scb[i][:, 0:width], in0=scb[i][:, 0:width],
                                  scalar1=0.25), [scb[i][:]], [scb[i][:]])
                    kb.op("DVE", lambda e, i=i, c=c, width=width:
                          e.scalar_tensor_tensor(
                              out=wout[:, c * width:(c + 1) * width],
                              in0=sqi[i][:, 0:width], scalar=128.0,
                              in1=scb[i][:, 0:width],
                              op0=ALU.subtract, op1=ALU.mult),
                          [sqi[i][:], scb[i][:]], [wout[:]])
                    stw.dma(wdram[:, c * width:(c + 1) * width],
                            wout[:, c * width:(c + 1) * width])

                jobs = ([(fc1q, fc1s, c, F, w1b, w1d, False) for c in range(NC)]
                        + [(fc2q, fc2s, c, D, w2b, w2d, True)
                           for c in range(FC)])
                dq_load(0, jobs)
                for n in range(len(jobs)):
                    if n + 1 < len(jobs):
                        dq_load(n + 1, jobs)
                    dq_proc(n, jobs)
                stw.wait_all("SP")
            ffn_phase(xT, x1d, w1b, w2b, "A")
        kb.barrier()

        esKT = contextlib.ExitStack()
        with esKT:
            kTs = esKT.enter_context(nc.sbuf_tensor("kTs", [128, NC * S], BF16))
            esB = contextlib.ExitStack()
            with esB:
                sbB = lambda n, sh, dt: esB.enter_context(nc.sbuf_tensor(n, sh, dt))
                psB = lambda n, sh: esB.enter_context(nc.psum_tensor(n, sh, F32))
                Wqb = sbB("Wqb", [128, NC * D], BF16)
                Wkb = sbB("Wkb", [128, NC * D], BF16)
                Wvb = sbB("Wvb", [128, NC * D], BF16)
                bvb = sbB("bvb", [128, D], F32)
                wtmp = [sbB("wtmp%d" % i, [128, D], F32) for i in (0, 1)]
                x_t = [sbB("bx_t%d" % i, [128, NC * T], F32) for i in (0, 1)]
                u2_t = sbB("bu2", [128, NC * T], F32)
                u2b = [sbB("bu2b%d" % i, [128, NC * T], BF16) for i in (0, 1)]
                xb = sbB("bxb", [128, NC * T], BF16)
                sq = sbB("bsq", [128, NC * T], BF16)
                q_b = sbB("bq_b", [128, NC * T], BF16)
                v_sb = sbB("bv_sb", [128, 1040], BF16)
                qsq = sbB("bqsq", [128, NC * T], BF16)
                n2m = sbB("bn2m", [16, 1], F32)
                pstat = [psB("bpstat%d" % i, [128, 512]) for i in (0, 1)]
                pp = [psB("bpp%d" % i, [128, 512]) for i in (0, 1)]
                pv = [psB("bpv%d" % i, [128, 512]) for i in (0, 1)]
                pqn = psB("bpqn", [16, 512])
                pkn = psB("bpkn", [16, 512])
                ldw = kb.stream("ldwB")
                lds = [kb.stream("ldb%d" % i) for i in (0, 1)]
                sts = [kb.stream("stb%d" % i) for i in (0, 1)]

                wjobs = [(Wd, Wsb, c) for Wd, Wsb in
                         ((Wq, Wqb), (Wk, Wkb), (Wv, Wvb)) for c in range(NC)]
                ldww = [kb.stream("ldww%d" % i) for i in (0, 1)]
                ldww[0].dma(wtmp[0][:], wjobs[0][0][0:128, :])
                for n, (Wd, Wsb, c) in enumerate(wjobs):
                    if n + 1 < len(wjobs):
                        W2, _, c2 = wjobs[n + 1]
                        ldww[(n + 1) % 2].dma(wtmp[(n + 1) % 2][:],
                                              W2[c2 * 128:(c2 + 1) * 128, :])
                    ldww[n % 2].wait_all("DVE")
                    w = wtmp[n % 2]
                    kb.op("DVE", lambda e, w=w, Wsb=Wsb, c=c: e.tensor_copy(
                        out=Wsb[:, c * D:(c + 1) * D], in_=w[:]),
                        [w[:]], [Wsb[:]])
                ldw.dma(bvb[:], bv.rearrange("(o d) -> o d", o=1)
                        .partition_broadcast(128))

                kb.op("DVE", lambda e: e.memset(v_sb[:], 1.0), [], [v_sb[:]])
                x1v, u2v, qv = fm(x1d), fm(u2d), fm(qTd)
                kv3 = kTs.ap().rearrange("p (c s) -> p c s", s=S)

                def loadB(t):
                    lds[t % 2].dma(c3(x_t[t % 2][:]), x1v[:, :, t * T:(t + 1) * T])

                def bp1(t):
                    x = x_t[t % 2]
                    ub = u2b[t % 2]
                    lds[t % 2].wait_all("POOL")
                    mean, rstd = ln_stats(x[:], xb, sq, pstat[0], 1)
                    kb.op("DVE", lambda e: e.tensor_tensor(
                        out=c3(u2_t[:]), in0=c3(x[:]), in1=bc(mean[:]),
                        op=ALU.subtract), [x[:], mean[:]], [u2_t[:]])
                    kb.op("DVE", lambda e, ub=ub: e.tensor_tensor(
                        out=c3(ub[:]), in0=c3(u2_t[:]), in1=bc(rstd[:]),
                        op=ALU.mult), [u2_t[:], rstd[:]], [ub[:]])
                    sts[t % 2].dma(u2v[:, :, t * T:(t + 1) * T], c3(ub[:]))

                def bp2q(t):
                    ub = u2b[t % 2]
                    for m in range(NC):
                        p = pp[m % 2]
                        kb.mms([(p[:, 0:T],
                                 Wqb[:, c * D + m * 128:c * D + (m + 1) * 128],
                                 ub[:, c * T:(c + 1) * T], c == 0, c == NC - 1)
                                for c in range(NC)])
                        kb.op("ACT", lambda e, p=p, m=m: e.activation(
                            out=q_b[:, m * T:(m + 1) * T], in_=p[:, 0:T],
                            func=AF.Identity, bias=bq_c[:, m:m + 1], scale=1.0),
                            [p[:], bq_c[:]], [q_b[:]])
                    sts[t % 2].dma(qv[:, :, t * T:(t + 1) * T], c3(q_b[:]))
                    kb.op("DVE", lambda e: e.tensor_mul(
                        out=qsq[:], in0=q_b[:], in1=q_b[:]), [q_b[:]], [qsq[:]])
                    kb.mms([(pqn[0:16, 0:T], sel_sb[:, c * 16:(c + 1) * 16],
                             qsq[:, c * T:(c + 1) * T], c == 0, c == NC - 1)
                            for c in range(NC)])
                    kb.op("DVE", lambda e: e.reduce_max(
                        out=n2m[:], in_=pqn[0:16, 0:T], axis=AX.X),
                        [pqn[:]], [n2m[:]])
                    if t == 0:
                        kb.op("DVE", lambda e: e.tensor_copy(
                            out=qn2run[:], in_=n2m[:]), [n2m[:]], [qn2run[:]])
                    else:
                        kb.op("DVE", lambda e: e.tensor_tensor(
                            out=qn2run[:], in0=qn2run[:], in1=n2m[:], op=ALU.max),
                            [qn2run[:], n2m[:]], [qn2run[:]])

                def bp2k(t):
                    ub = u2b[t % 2]
                    for m in range(NC):
                        p = pp[m % 2]
                        kb.mms([(p[:, 0:T],
                                 Wkb[:, c * D + m * 128:c * D + (m + 1) * 128],
                                 ub[:, c * T:(c + 1) * T], c == 0, c == NC - 1)
                                for c in range(NC)])
                        kb.op("ACT", lambda e, p=p, m=m, t=t: e.activation(
                            out=kTs[:, m * S + t * T:m * S + (t + 1) * T],
                            in_=p[:, 0:T], func=AF.Identity,
                            bias=bk_c[:, m:m + 1], scale=1.0),
                            [p[:], bk_c[:]], [kTs[:]])
                    kb.op("DVE", lambda e, t=t: e.tensor_mul(
                        out=c3(qsq[:]), in0=kv3[:, :, t * T:(t + 1) * T],
                        in1=kv3[:, :, t * T:(t + 1) * T]), [kTs[:]], [qsq[:]])
                    kb.mms([(pkn[0:16, 0:T], sel_sb[:, c * 16:(c + 1) * 16],
                             qsq[:, c * T:(c + 1) * T], c == 0, c == NC - 1)
                            for c in range(NC)])
                    kb.op("DVE", lambda e: e.reduce_max(
                        out=n2m[:], in_=pkn[0:16, 0:T], axis=AX.X),
                        [pkn[:]], [n2m[:]])
                    if t == 0:
                        kb.op("DVE", lambda e: e.tensor_copy(
                            out=kn2run[:], in_=n2m[:]), [n2m[:]], [kn2run[:]])
                    else:
                        kb.op("DVE", lambda e: e.tensor_tensor(
                            out=kn2run[:], in0=kn2run[:], in1=n2m[:], op=ALU.max),
                            [kn2run[:], n2m[:]], [kn2run[:]])

                def bp2v(t):
                    ub = u2b[t % 2]
                    for sub in range(T // 128):
                        for dvc in range(2):
                            p = pv[dvc]
                            kb.mms([(p[:, 0:512],
                                     ub[:, c * T + sub * 128:c * T + (sub + 1) * 128],
                                     Wvb[:, c * D + dvc * 512:c * D + (dvc + 1) * 512],
                                     c == 0, c == NC - 1) for c in range(NC)])
                            vout = v_sb.ap()[:, dvc * 520:dvc * 520 + 520] \
                                .rearrange("p (h c) -> p h c", c=65)[:, 0:8, 0:64]
                            kb.op("DVE", lambda e, p=p, dvc=dvc, vout=vout:
                                  e.tensor_tensor(
                                      out=vout,
                                      in0=p.ap()[:, 0:512].rearrange(
                                          "p (h c) -> p h c", c=64),
                                      in1=bvb.ap()[:, dvc * 512:(dvc + 1) * 512]
                                      .rearrange("p (h c) -> p h c", c=64),
                                      op=ALU.add),
                                  [p[:], bvb[:]], [v_sb[:]])
                        row0 = t * T + sub * 128
                        sts[t % 2].dma(vd[row0:row0 + 128, :], v_sb[:])

                loadB(0)
                bp1(0)
                for t in range(nT):
                    if t + 1 < nT:
                        loadB(t + 1)
                    bp2q(t)
                    if t + 1 < nT:
                        bp1(t + 1)
                    bp2k(t)
                    bp2v(t)
                kb.op("DVE", lambda e: e.tensor_mul(out=qn2run[:], in0=qn2run[:],
                                                    in1=kn2run[:]),
                      [qn2run[:], kn2run[:]], [qn2run[:]])
                kb.op("ACT", lambda e: e.activation(out=qn2run[:], in_=qn2run[:],
                                                    func=AF.Sqrt, bias=0.0,
                                                    scale=1.0 / 64.0),
                      [qn2run[:]], [qn2run[:]])
                kb.op("PE", lambda e: e.transpose(out=pstat[0][0:1, 0:16],
                                                  in_=qn2run[:],
                                                  identity=id16_sb[:]),
                      [qn2run[:], id16_sb[:]], [pstat[0][:]])
                kb.op("DVE", lambda e: e.tensor_scalar_mul(
                    out=negbrow[:], in0=pstat[0][0:1, 0:16], scalar1=-1.0),
                    [pstat[0][:]], [negbrow[:]])
                kb.op("DVE", lambda e: e.tensor_tensor(
                    out=negbrow[:, 0:8], in0=negbrow[0:1, 0:16:2],
                    in1=negbrow[0:1, 1:16:2], op=ALU.min),
                    [negbrow[:]], [negbrow[:]])
                kb.mms([(pstat[1][0:128, 0:8], ones_f[:], negbrow[:, 0:8],
                         True, True)])
                kb.op("DVE", lambda e: e.tensor_copy(out=negb_c[:],
                                                     in_=pstat[1][0:128, 0:8]),
                      [pstat[1][:]], [negb_c[:]])
                for s in sts:
                    s.wait_all("SP")
            kb.barrier()

            esT = contextlib.ExitStack()
            with esT:
                sbT = lambda n, sh, dt: esT.enter_context(nc.sbuf_tensor(n, sh, dt))
                psT = lambda n, sh: esT.enter_context(nc.psum_tensor(n, sh, F32))
                v65 = sbT("v65", [128, nKB * 1040], BF16)
                qTg = [sbT("qTg%d" % i, [128, NC * QG], BF16) for i in (0, 1)]
                PT = [sbT("PT%d" % i, [128, 1024], BF16) for i in (0, 1)]
                oTg = [sbT("oTg%d" % i, [128, NC * QG], BF16) for i in (0, 1)]
                ssb = [sbT("ssb%d" % i, [65, 1024], F32) for i in (0, 1)]
                pS = [psT("pS%d" % i, [128, 1024]) for i in (0, 1)]
                pAV = [psT("pAV%d" % i, [128, 512]) for i in range(4)]
                ldv = kb.stream("ldv")
                ldqg = [kb.stream("ldqg%d" % i) for i in (0, 1)]
                sto = kb.stream("sto")

                vsrc = vd.rearrange("(b p) c -> p b c", p=128)
                vdst = v65.ap().rearrange("p (b c) -> p b c", c=1040)
                for j in range(4):
                    blk = nKB // 4
                    ldv.dma(vdst[:, j * blk:(j + 1) * blk, :],
                            vsrc[:, j * blk:(j + 1) * blk, :])
                qgv = fm(qTd)

                def loadG(g):
                    ldqg[g % 2].dma(
                        qTg[g % 2].ap().rearrange("p (c s) -> p c s", s=QG),
                        qgv[:, :, g * QG:(g + 1) * QG])

                loadG(0)
                for g in range(nG):
                    if g + 1 < nG:
                        loadG(g + 1)
                    qg = qTg[g % 2]
                    ldqg[g % 2].wait_all("PE")
                    ldv.wait_all("PE")
                    kbmax = 4 * g + 3
                    units = [(c, kbi) for c in range(8)
                             for kbi in range(kbmax + 1)]

                    def emit_S(idx):
                        c, kbi = units[idx]
                        p = pS[idx % 2]
                        lA = kTs[0:64, c * S + kbi * 128:c * S + (kbi + 1) * 128]
                        lB = kTs[64:128, c * S + kbi * 128:c * S + (kbi + 1) * 128]
                        rA = qg[0:64, c * QG:(c + 1) * QG]
                        rB = qg[64:128, c * QG:(c + 1) * QG]
                        m = kbi - 4 * g
                        items = []
                        for off, l, r in ((0, lA, rA), (512, lB, rB)):
                            if 0 <= m:
                                items += [(p[:, off:off + 512], l, r,
                                           True, False),
                                          (p[:, off + m * 128:
                                             off + (m + 1) * 128],
                                           id128_sb[:], mask_sb[:],
                                           False, True)]
                            else:
                                items.append((p[:, off:off + 512], l, r,
                                              True, True))
                        kb.mms(items)

                    def emit_exp(idx):
                        c, kbi = units[idx]
                        p = pS[idx % 2]
                        pt = PT[idx % 2]
                        kb.op("ACT", lambda e, p=p, pt=pt, c=c: e.activation(
                            out=pt[:], in_=p[:], func=AF.Exp,
                            bias=negb_c[:, c:c + 1], scale=0.125),
                            [p[:], negb_c[:]], [pt[:]])

                    def emit_AV(idx):
                        c, kbi = units[idx]
                        pt = PT[idx % 2]
                        pavA, pavB = pAV[2 * (c % 2)], pAV[2 * (c % 2) + 1]
                        hA, hB = 2 * c, 2 * c + 1
                        m = max(kbi - 4 * g, 0)
                        q0 = m * 128
                        vA = v65[:, kbi * 1040 + hA * 65:
                                 kbi * 1040 + hA * 65 + 65]
                        vB = v65[:, kbi * 1040 + hB * 65:
                                 kbi * 1040 + hB * 65 + 65]
                        kb.mms([(pavA[0:65, q0:QG], vA, pt[:, q0:512],
                                 kbi == 0, kbi == kbmax),
                                (pavB[0:65, q0:QG], vB, pt[:, 512 + q0:1024],
                                 kbi == 0, kbi == kbmax)])

                    def emit_tail(c):
                        og = oTg[g % 2]
                        sb_ = ssb[c % 2]
                        pavA, pavB = pAV[2 * (c % 2)], pAV[2 * (c % 2) + 1]
                        for hh, pav in ((0, pavA), (1, pavB)):
                            rows = slice(0, 64) if hh == 0 else slice(64, 128)
                            kb.op("DVE", lambda e, pav=pav, rows=rows, c=c,
                                  og=og: e.tensor_copy(
                                      out=og[rows, c * QG:(c + 1) * QG],
                                      in_=pav[0:64, 0:QG]),
                                  [pav[:]], [og[:]])
                            kb.op("DVE", lambda e, pav=pav, sb_=sb_, hh=hh:
                                  e.tensor_copy(
                                      out=sb_[64:65, hh * QG:(hh + 1) * QG],
                                      in_=pav[64:65, 0:QG]),
                                  [pav[:]], [sb_[:]])
                            sto.dma(sumd[2 * c + hh:2 * c + hh + 1,
                                         g * QG:(g + 1) * QG],
                                    sb_[64:65, hh * QG:(hh + 1) * QG])

                    emit_S(0)
                    for idx in range(len(units)):
                        emit_exp(idx)
                        if idx + 1 < len(units):
                            emit_S(idx + 1)
                        emit_AV(idx)
                        c, kbi = units[idx]
                        if kbi == kbmax:
                            emit_tail(c)
                    sto.dma(fm(oTd)[:, :, g * QG:(g + 1) * QG],
                            oTg[g % 2].ap().rearrange("p (c s) -> p c s", s=QG))
                sto.wait_all("SP")
        kb.barrier()

        esE = contextlib.ExitStack()
        with esE:
            sbE = lambda n, sh, dt: esE.enter_context(nc.sbuf_tensor(n, sh, dt))
            psE = lambda n, sh: esE.enter_context(nc.psum_tensor(n, sh, F32))
            Wob = sbE("Wob", [128, NC * D], BF16)
            wtmp = [sbE("ewtmp%d" % i, [128, D], F32) for i in (0, 1)]
            o_t = [sbE("eo_t%d" % i, [128, NC * T], BF16) for i in (0, 1)]
            u2_t = [sbE("eu2%d" % i, [128, NC * T], BF16) for i in (0, 1)]
            x1_t = [sbE("ex1%d" % i, [128, NC * T], F32) for i in (0, 1)]
            y_t = [sbE("ey_t%d" % i, [128, NC * T], F32) for i in (0, 1)]
            xb = sbE("exb", [128, NC * T], BF16)
            sq = sbE("esq", [128, NC * T], BF16)
            ssb2 = sbE("essum", [16, S], F32)
            rr_b = sbE("err_b", [16, S], BF16)
            pstat = [psE("epstat%d" % i, [128, 512]) for i in (0, 1)]
            po = [psE("epo%d" % i, [128, 512]) for i in (0, 1)]
            pR = [psE("epR%d" % i, [128, 512]) for i in (0, 1)]
            ldw = kb.stream("ldwE")
            lds = [kb.stream("lde%d" % i) for i in (0, 1)]
            sts = [kb.stream("ste%d" % i) for i in (0, 1)]

            ldsm = kb.stream("ldsm")
            ldsm.dma(ssb2[:], sumd)
            ldsm.wait_all("DVE")
            kb.op("DVE", lambda e: e.reciprocal(out=ssb2[:], in_=ssb2[:]),
                  [ssb2[:]], [ssb2[:]])
            kb.op("DVE", lambda e: e.tensor_copy(out=rr_b[:], in_=ssb2[:]),
                  [ssb2[:]], [rr_b[:]])
            ldwe = [kb.stream("ldwe%d" % i) for i in (0, 1)]
            ldwe[0].dma(wtmp[0][:], Wo[0:128, :])
            for c in range(NC):
                if c + 1 < NC:
                    ldwe[(c + 1) % 2].dma(wtmp[(c + 1) % 2][:],
                                          Wo[(c + 1) * 128:(c + 2) * 128, :])
                ldwe[c % 2].wait_all("DVE")
                w = wtmp[c % 2]
                kb.op("DVE", lambda e, w=w, c=c: e.tensor_copy(
                    out=Wob[:, c * D:(c + 1) * D], in_=w[:]), [w[:]], [Wob[:]])
            ov, u2v, x1v, x2v = fm(oTd), fm(u2d), fm(x1d), fm(x2d)

            def loadE(t):
                i = t % 2
                lds[i].dma(c3(o_t[i][:]), ov[:, :, t * T:(t + 1) * T])
                lds[i].dma(c3(u2_t[i][:]), u2v[:, :, t * T:(t + 1) * T])
                lds[i].dma(c3(x1_t[i][:]), x1v[:, :, t * T:(t + 1) * T])


            def ep1(t):
                i = t % 2
                lds[i].wait_all("PE")
                lds[i].wait_all("DVE")
                for m in range(NC):
                    pr = pR[m % 2]
                    kb.mms([(pr[:, 0:T], selh_sb[:, m * 128:(m + 1) * 128],
                             rr_b[0:16, t * T:(t + 1) * T], True, True)])
                    kb.op("DVE", lambda e, i=i, m=m, pr=pr: e.tensor_mul(
                        out=o_t[i][:, m * T:(m + 1) * T],
                        in0=o_t[i][:, m * T:(m + 1) * T], in1=pr[:, 0:T]),
                        [o_t[i][:], pr[:]], [o_t[i][:]])
                for m in range(NC):
                    p = po[m % 2]
                    kb.mms([(p[:, 0:T],
                             Wob[:, c * D + m * 128:c * D + (m + 1) * 128],
                             o_t[i][:, c * T:(c + 1) * T], c == 0, c == NC - 1)
                            for c in range(NC)])
                    kb.op("DVE", lambda e, p=p, m=m, i=i: e.scalar_tensor_tensor(
                        out=y_t[i][:, m * T:(m + 1) * T], in0=p[:, 0:T],
                        scalar=bo_c[:, m:m + 1], in1=u2_t[i][:, m * T:(m + 1) * T],
                        op0=ALU.add, op1=ALU.add),
                        [p[:], bo_c[:], u2_t[i][:]], [y_t[i][:]])

            def ep2(t):
                i = t % 2
                y, x1 = y_t[i], x1_t[i]
                mean, rstd = ln_stats(y[:], xb, sq, pstat[0], 1)
                kb.op("POOL", lambda e, y=y: e.tensor_tensor(
                    out=c3(y[:]), in0=c3(y[:]), in1=bc(mean[:]),
                    op=ALU.subtract), [y[:], mean[:]], [y[:]])
                kb.op("POOL", lambda e, y=y: e.tensor_tensor(
                    out=c3(y[:]), in0=c3(y[:]), in1=bc(rstd[:]),
                    op=ALU.mult), [y[:], rstd[:]], [y[:]])
                for m in range(NC):
                    ms = slice(m * T, (m + 1) * T)
                    kb.op("DVE", lambda e, m=m, ms=ms, x1=x1, y=y:
                          e.scalar_tensor_tensor(
                              out=x1[:, ms], in0=y[:, ms],
                              scalar=agam_c[:, m:m + 1], in1=x1[:, ms],
                              op0=ALU.mult, op1=ALU.add),
                          [y[:], agam_c[:], x1[:]], [x1[:]])
                    kb.op("DVE", lambda e, m=m, ms=ms, x1=x1: e.tensor_scalar(
                        out=x1[:, ms], in0=x1[:, ms], scalar1=abet_c[:, m:m + 1],
                        scalar2=None, op0=ALU.add),
                        [x1[:], abet_c[:]], [x1[:]])
                sts[i].dma(x2v[:, :, t * T:(t + 1) * T], c3(x1[:]))

            loadE(0)
            ep1(0)
            for t in range(nT):
                if t + 1 < nT:
                    loadE(t + 1)
                ep2(t)
                if t + 1 < nT:
                    ep1(t + 1)
            for s in sts:
                s.wait_all("SP")
        kb.barrier()

        esC = contextlib.ExitStack()
        with esC:
            w1b = esC.enter_context(nc.sbuf_tensor("w1bC", [128, NC * F], BF16))
            w2b = esC.enter_context(nc.sbuf_tensor("w2bC", [128, FC * D], BF16))
            ldw = kb.stream("ldwC")
            for j in range(4):
                w = NC * F // 4
                ldw.dma(w1b[:, j * w:(j + 1) * w], w1d[:, j * w:(j + 1) * w])
                ldw.dma(w2b[:, j * w:(j + 1) * w], w2d[:, j * w:(j + 1) * w])
            ldw.wait_all("PE")
            ffn_phase(x2d, outT, w1b, w2b, "C")
    return nc


def kernel(**inputs):
    S = 4096
    x = np.asarray(inputs["L0_attn_x"], dtype=np.float32)
    nm = {
        "Wq": "L0_attn_Wq", "bq": "L0_attn_bq", "Wk": "L0_attn_Wk",
        "bk": "L0_attn_bk", "Wv": "L0_attn_Wv", "bv": "L0_attn_bv",
        "Wo": "L0_attn_Wo", "bo": "L0_attn_bo", "agam": "L0_attn_gamma",
        "abet": "L0_attn_beta", "lgam": "L0_ffn_ln_gamma",
        "lbet": "L0_ffn_ln_beta", "fc1s": "L0_ffn_fc1_scales",
        "fc1b": "L0_ffn_fc1_bias", "fc2s": "L0_ffn_fc2_scales",
        "fc2b": "L0_ffn_fc2_bias",
    }
    base = {k: np.ascontiguousarray(np.asarray(inputs[v], dtype=np.float32))
            for k, v in nm.items()}
    base["fc1q"] = np.ascontiguousarray(np.asarray(inputs["L0_ffn_fc1_q"],
                                                   dtype=np.int32))
    base["fc2q"] = np.ascontiguousarray(np.asarray(inputs["L0_ffn_fc2_q"],
                                                   dtype=np.int32))
    in_maps = []
    for b in range(8):
        m = dict(base)
        m["xT"] = np.ascontiguousarray(x[b].T)
        in_maps.append(m)
    nc = build_nc(S)
    res = run_bass_kernel_spmd(nc, in_maps, core_ids=list(range(8)))
    out = np.stack([np.asarray(r["outT"]).T for r in res.results], axis=0)
    return out.astype(np.float32)


# version 13
# speedup vs baseline: 1.0225x; 1.0083x over previous
import contextlib
import numpy as np

try:
    import concourse.bass as bass
except ImportError:
    import sys
    sys.path.insert(0, "/opt/trn_rl_repo")
    import concourse.bass as bass
import concourse.mybir as mybir
from concourse.bass_utils import run_bass_kernel_spmd

F32 = mybir.dt.float32
BF16 = mybir.dt.bfloat16
I32 = mybir.dt.int32
AF = mybir.ActivationFunctionType
ALU = mybir.AluOpType
AX = mybir.AxisListType

D = 1024
F = 4096
H = 16
HD = 64
NC = 8
FC = 32
T = 256
QG = 512
EPS = 1e-5
NEG = -1.0e30


class KB:

    def __init__(self, nc, es):
        self.nc = nc
        self.es = es
        self.eng = dict(PE=nc.tensor, ACT=nc.scalar, DVE=nc.vector,
                        POOL=nc.gpsimd, SP=nc.sync)
        self.sem = {k: es.enter_context(nc.semaphore("e_" + k)) for k in self.eng}
        self.cnt = {k: 0 for k in self.eng}
        self.wr = {}
        self.rd = {}
        self.waited = {}
        self.semobj = {k: self.sem[k] for k in self.eng}
        self.streams = {}

    def _wait(self, eng, semkey, val):
        if semkey == eng:
            return
        if semkey in self.streams:
            val = 16 * self.streams[semkey].issued
        else:
            assert val is not None and val <= self.cnt[semkey], (semkey, val)
        w = (eng, semkey)
        if self.waited.get(w, -1) >= val:
            return
        self.eng[eng].wait_ge(self.semobj[semkey], val)
        self.waited[w] = val

    def dep(self, eng, rkeys, wkeys):
        for k in rkeys:
            if k in self.wr:
                s, v = self.wr[k]
                self._wait(eng, s, v)
        for k in wkeys:
            if k in self.wr:
                s, v = self.wr[k]
                self._wait(eng, s, v)
            for s, v in self.rd.get(k, {}).items():
                self._wait(eng, s, v)

    def _note(self, eng_semkey, val, rkeys, wkeys):
        for k in wkeys:
            self.wr[k] = (eng_semkey, val)
            self.rd[k] = {}
        for k in rkeys:
            self.rd.setdefault(k, {})[eng_semkey] = val

    @staticmethod
    def keys(aps):
        return [a.name for a in aps]

    def op(self, eng, fn, reads, writes):
        rk, wk = self.keys(reads), self.keys(writes)
        self.dep(eng, rk, wk)
        ins = fn(self.eng[eng])
        self.cnt[eng] += 1
        ins.then_inc(self.sem[eng], 1)
        self._note(eng, self.cnt[eng], rk, wk)
        return ins

    def mms(self, items):
        rk, wk = [], []
        for o, l, r, st, sp in items:
            rk += self.keys([l, r])
            wk.append(o.name)
        self.dep("PE", rk, wk)
        for o, l, r, st, sp in items:
            ins = self.eng["PE"].matmul(out=o, lhsT=l, rhs=r, start=st, stop=sp)
        self.cnt["PE"] += 1
        ins.then_inc(self.sem["PE"], 1)
        self._note("PE", self.cnt["PE"], rk, wk)

    def barrier(self):
        self.nc.all_engine_barrier()
        self.wr = {k: v for k, v in self.wr.items() if v[0] in self.streams}
        self.rd = {k: {s: v for s, v in d.items() if s in self.streams}
                   for k, d in self.rd.items()}
        self.rd = {k: d for k, d in self.rd.items() if d}
        self.waited = {}

    def stream(self, name):
        s = DmaStream(self, name, self.es)
        self.streams[name] = s
        self.semobj[name] = s.sem
        return s


class DmaStream:
    def __init__(self, kb, name, es):
        self.kb = kb
        self.name = name
        self.sem = es.enter_context(kb.nc.semaphore("dma_" + name))
        self.issued = 0

    def dma(self, out, in_, **kw):
        kb = self.kb
        rk, wk = [in_.name], [out.name]
        kb.dep("SP", rk, wk)
        ins = kb.eng["SP"].dma_start(out=out, in_=in_, **kw)
        ins.then_inc(self.sem, 16)
        self.issued += 1
        kb._note(self.name, None, rk, wk)

    def wait_all(self, eng):
        self.kb._wait(eng, self.name, None)


def build_nc(S):
    nT = S // T
    nG = S // QG
    nKB = S // 128
    nc = bass.Bass("TRN2", target_bir_lowering=False,
                   detect_race_conditions=False)

    dt_in = lambda n, sh, dt: nc.dram_tensor(n, sh, dt, kind="ExternalInput").ap()
    dt_int = lambda n, sh, dt: nc.dram_tensor(n, sh, dt, kind="Internal").ap()

    xT = dt_in("xT", [D, S], F32)
    Wq, Wk, Wv, Wo = (dt_in(n, [D, D], F32) for n in ("Wq", "Wk", "Wv", "Wo"))
    bq, bk, bv, bo = (dt_in(n, [D], F32) for n in ("bq", "bk", "bv", "bo"))
    agam, abet = dt_in("agam", [D], F32), dt_in("abet", [D], F32)
    lgam, lbet = dt_in("lgam", [D], F32), dt_in("lbet", [D], F32)
    fc1q = dt_in("fc1q", [D, F], I32)
    fc1s = dt_in("fc1s", [NC, F], F32)
    fc1b = dt_in("fc1b", [F], F32)
    fc2q = dt_in("fc2q", [F, D], I32)
    fc2s = dt_in("fc2s", [FC, D], F32)
    fc2b = dt_in("fc2b", [D], F32)
    outT = nc.dram_tensor("outT", [D, S], F32, kind="ExternalOutput").ap()

    x1d = dt_int("x1d", [D, S], F32)
    u2d = dt_int("u2d", [D, S], BF16)
    x2d = dt_int("x2d", [D, S], F32)
    qTd = dt_int("qTd", [D, S], BF16)
    vd = dt_int("vd", [S, 1040], BF16)
    oTd = dt_int("oTd", [D, S], BF16)
    sumd = dt_int("sumd", [H, S], F32)
    w1d = dt_int("w1d", [128, NC * F], BF16)
    w2d = dt_int("w2d", [128, FC * D], BF16)

    import ml_dtypes
    mask_np = np.where(np.arange(128)[:, None] > np.arange(128)[None, :],
                       np.float32(NEG), np.float32(0.0))
    maskc = nc.inline_tensor(mask_np.astype(ml_dtypes.bfloat16), name="maskc").ap()
    id128c = nc.inline_tensor(np.eye(128, dtype=np.float32)
                              .astype(ml_dtypes.bfloat16), name="id128c").ap()
    sel_np = np.zeros((128, NC * 16), dtype=np.float32)
    for c in range(NC):
        sel_np[0:64, c * 16 + 2 * c] = 1.0
        sel_np[64:128, c * 16 + 2 * c + 1] = 1.0
    selc = nc.inline_tensor(sel_np.astype(ml_dtypes.bfloat16), name="selc").ap()
    selh_np = np.zeros((16, NC * 128), dtype=np.float32)
    for c in range(NC):
        selh_np[2 * c, c * 128:c * 128 + 64] = 1.0
        selh_np[2 * c + 1, c * 128 + 64:(c + 1) * 128] = 1.0
    selhc = nc.inline_tensor(selh_np.astype(ml_dtypes.bfloat16), name="selhc").ap()
    id16 = nc.inline_tensor(np.eye(16, dtype=np.float32), name="id16").ap()

    def fm(dram):
        return dram.rearrange("(c p) s -> p c s", p=128)

    es0 = contextlib.ExitStack()
    with es0:
        kb = KB(nc, es0)
        sb = lambda n, sh, dt: es0.enter_context(nc.sbuf_tensor(n, sh, dt))

        mask_sb = sb("mask_sb", [128, 128], BF16)
        id128_sb = sb("id128_sb", [128, 128], BF16)
        sel_sb = sb("sel_sb", [128, NC * 16], BF16)
        id16_sb = sb("id16_sb", [16, 16], F32)
        selh_sb = sb("selh_sb", [16, NC * 128], BF16)
        lgam_c = sb("lgam_c", [128, NC], F32)
        lbet_c = sb("lbet_c", [128, NC], F32)
        agam_c = sb("agam_c", [128, NC], F32)
        abet_c = sb("abet_c", [128, NC], F32)
        bq_c = sb("bq_c", [128, NC], F32)
        bk_c = sb("bk_c", [128, NC], F32)
        bo_c = sb("bo_c", [128, NC], F32)
        b1_c = sb("b1_c", [128, FC], F32)
        b2_c = sb("b2_c", [128, NC], F32)
        onesP = sb("onesP", [128, 128], BF16)
        eps_c = sb("eps_c", [128, 1], F32)
        ones_f = sb("ones_f", [1, 128], F32)
        quart_f = sb("quart_f", [1, 128], F32)
        ones_b = sb("ones_b", [1, 128], BF16)
        qn2run = sb("qn2run", [16, 1], F32)
        kn2run = sb("kn2run", [16, 1], F32)
        negbrow = sb("negbrow", [1, 16], F32)
        negb_c = sb("negb_c", [128, 8], F32)
        stat = {n: sb("st_" + n, [128, T], F32)
                for n in ("mean", "v", "r", "mean2", "v2", "r2")}

        ld0 = kb.stream("ld0")
        colv = lambda v: v.rearrange("(m p) -> p m", p=128)
        for dst, src in ((lgam_c, lgam), (lbet_c, lbet), (agam_c, agam),
                         (abet_c, abet), (bq_c, bq), (bk_c, bk), (bo_c, bo),
                         (b1_c, fc1b), (b2_c, fc2b)):
            ld0.dma(dst[:], colv(src), allow_slow_non_contiguous=True)
        ld0.dma(mask_sb[:], maskc)
        ld0.dma(id128_sb[:], id128c)
        ld0.dma(sel_sb[:], selc)
        ld0.dma(selh_sb[:], selhc)
        ld0.dma(id16_sb[:], id16)
        kb.op("DVE", lambda e: e.memset(onesP[:], 1.0 / 1024.0), [], [onesP[:]])
        kb.op("DVE", lambda e: e.memset(eps_c[:], EPS), [], [eps_c[:]])
        kb.op("DVE", lambda e: e.memset(ones_f[:], 1.0), [], [ones_f[:]])
        kb.op("DVE", lambda e: e.memset(quart_f[:], 0.25), [], [quart_f[:]])
        kb.op("DVE", lambda e: e.memset(ones_b[:], 1.0), [], [ones_b[:]])
        ld0.wait_all("DVE")
        kb.op("DVE", lambda e: e.tensor_scalar_mul(out=b2_c[:], in0=b2_c[:],
                                                   scalar1=0.25),
              [b2_c[:]], [b2_c[:]])

        def ln_stats(src, xb, sq, pstat, sfx):
            mean, var, rstd = (stat[k + ("" if sfx == 1 else "2")]
                               for k in ("mean", "v", "r"))
            kb.op("DVE", lambda e: e.tensor_copy(out=xb[:], in_=src[:]),
                  [src[:]], [xb[:]])
            kb.op("DVE", lambda e: e.tensor_mul(out=sq[:], in0=xb[:], in1=xb[:]),
                  [xb[:]], [sq[:]])
            kb.mms([(pstat[:, 0:T], onesP[:], xb[:, c * T:(c + 1) * T],
                     c == 0, c == NC - 1) for c in range(NC)])
            kb.mms([(pstat[:, T:2 * T], onesP[:], sq[:, c * T:(c + 1) * T],
                     c == 0, c == NC - 1) for c in range(NC)])
            kb.op("DVE", lambda e: e.tensor_copy(out=mean[:], in_=pstat[:, 0:T]),
                  [pstat[:]], [mean[:]])
            kb.op("DVE", lambda e: e.tensor_mul(out=var[:], in0=mean[:], in1=mean[:]),
                  [mean[:]], [var[:]])
            kb.op("DVE", lambda e: e.tensor_tensor(out=var[:], in0=pstat[:, T:2 * T],
                                                   in1=var[:], op=ALU.subtract),
                  [pstat[:], var[:]], [var[:]])
            kb.op("ACT", lambda e: e.activation(out=rstd[:], in_=var[:], func=AF.Sqrt,
                                                bias=eps_c[:], scale=1.0),
                  [var[:], eps_c[:]], [rstd[:]])
            kb.op("DVE", lambda e: e.reciprocal(out=rstd[:], in_=rstd[:]),
                  [rstd[:]], [rstd[:]])
            return mean, rstd

        def ln_stats_pre(xb_t, sq, pstat, sfx):
            mean, var, rstd = (stat[k + ("" if sfx == 1 else "2")]
                               for k in ("mean", "v", "r"))
            kb.mms([(pstat[:, 0:T], onesP[:], xb_t[:, c * T:(c + 1) * T],
                     c == 0, c == NC - 1) for c in range(NC)])
            kb.mms([(pstat[:, T:2 * T], onesP[:], sq[:, c * T:(c + 1) * T],
                     c == 0, c == NC - 1) for c in range(NC)])
            kb.op("DVE", lambda e: e.tensor_copy(out=mean[:], in_=pstat[:, 0:T]),
                  [pstat[:]], [mean[:]])
            kb.op("DVE", lambda e: e.tensor_mul(out=var[:], in0=mean[:], in1=mean[:]),
                  [mean[:]], [var[:]])
            kb.op("DVE", lambda e: e.tensor_tensor(out=var[:], in0=pstat[:, T:2 * T],
                                                   in1=var[:], op=ALU.subtract),
                  [pstat[:], var[:]], [var[:]])
            kb.op("ACT", lambda e: e.activation(out=rstd[:], in_=var[:], func=AF.Sqrt,
                                                bias=eps_c[:], scale=1.0),
                  [var[:], eps_c[:]], [rstd[:]])
            kb.op("DVE", lambda e: e.reciprocal(out=rstd[:], in_=rstd[:]),
                  [rstd[:]], [rstd[:]])
            return mean, rstd

        def bc(ap):
            return ap.rearrange("p (o s) -> p o s", o=1).broadcast_to([128, NC, T])

        def c3(ap):
            return ap.rearrange("p (c s) -> p c s", s=T)

        def ffn_phase(src_dram, dst_dram, w1b, w2b, ph):
            esA = contextlib.ExitStack()
            with esA:
                sbA = lambda n, sh, dt: esA.enter_context(
                    nc.sbuf_tensor(n + ph, sh, dt))
                psA = lambda n, sh: esA.enter_context(
                    nc.psum_tensor(n + ph, sh, F32))
                x_t = [sbA("x_t%d" % i, [128, NC * T], F32) for i in (0, 1)]
                u_t = sbA("u_t", [128, NC * T], BF16)
                hf = sbA("hf", [128, NC * T], F32)
                h_b = [sbA("h_b%d" % i, [128, NC * T], BF16) for i in (0, 1)]
                xb = sbA("xb", [128, NC * T], BF16)
                sq = sbA("sq", [128, NC * T], BF16)
                y_b = sbA("y_b", [128, FC * T], BF16)
                pstat = [psA("pstat%d" % i, [128, 512]) for i in (0, 1)]
                py = [psA("py%d" % i, [128, 512]) for i in (0, 1)]
                pz = [psA("pz%d" % i, [128, 512]) for i in (0, 1)]
                lds = [kb.stream("ldx%d" % i + ph) for i in (0, 1)]
                sts = [kb.stream("stx%d" % i + ph) for i in (0, 1)]
                srcv, dstv = fm(src_dram), fm(dst_dram)

                def load(t):
                    lds[t % 2].dma(c3(x_t[t % 2][:]), srcv[:, :, t * T:(t + 1) * T])

                def p1a(t):
                    x = x_t[t % 2]
                    lds[t % 2].wait_all("POOL")
                    mean, rstd = ln_stats(x[:], xb, sq, pstat[0], 1)
                    kb.op("DVE", lambda e: e.tensor_tensor(
                        out=c3(u_t[:]), in0=c3(x[:]), in1=bc(mean[:]),
                        op=ALU.subtract), [x[:], mean[:]], [u_t[:]])
                    kb.op("DVE", lambda e: e.tensor_tensor(
                        out=c3(u_t[:]), in0=c3(u_t[:]), in1=bc(rstd[:]),
                        op=ALU.mult), [u_t[:], rstd[:]], [u_t[:]])

                def p1b(t):
                    x = x_t[t % 2]
                    hb = h_b[t % 2]
                    kb.op("DVE", lambda e: e.tensor_mul(out=sq[:], in0=u_t[:],
                                                        in1=u_t[:]),
                          [u_t[:]], [sq[:]])
                    mean2, rstd2 = ln_stats_pre(u_t, sq, pstat[1], 2)
                    kb.op("DVE", lambda e: e.tensor_tensor(
                        out=c3(hf[:]), in0=c3(u_t[:]), in1=bc(mean2[:]),
                        op=ALU.subtract), [u_t[:], mean2[:]], [hf[:]])
                    kb.op("DVE", lambda e: e.tensor_tensor(
                        out=c3(hf[:]), in0=c3(hf[:]), in1=bc(rstd2[:]),
                        op=ALU.mult), [hf[:], rstd2[:]], [hf[:]])
                    for c in range(NC):
                        cs = slice(c * T, (c + 1) * T)
                        kb.op("DVE", lambda e, c=c, cs=cs: e.tensor_scalar(
                            out=hb[:, cs], in0=hf[:, cs],
                            scalar1=lgam_c[:, c:c + 1], scalar2=lbet_c[:, c:c + 1],
                            op0=ALU.mult, op1=ALU.add),
                            [hf[:], lgam_c[:], lbet_c[:]], [hb[:]])
                    kb.op("DVE", lambda e: e.scalar_tensor_tensor(
                        out=x_t[t % 2][:], in0=u_t[:], scalar=0.5, in1=x[:],
                        op0=ALU.mult, op1=ALU.add), [u_t[:], x[:]], [x[:]])

                def p2y(t):
                    hb = h_b[t % 2]
                    for m in range(FC):
                        p = py[m % 2]
                        kb.mms([(p[:, 0:T],
                                 w1b[:, c * F + m * 128:c * F + (m + 1) * 128],
                                 hb[:, c * T:(c + 1) * T], c == 0, c == NC - 1)
                                for c in range(NC)])
                        kb.op("ACT", lambda e, p=p, m=m: e.activation(
                            out=y_b[:, m * T:(m + 1) * T], in_=p[:, 0:T],
                            func=AF.Silu, bias=b1_c[:, m:m + 1], scale=1.0),
                            [p[:], b1_c[:]], [y_b[:]])

                def p2z(t, ms):
                    x = x_t[t % 2]
                    for m in ms:
                        p = pz[m % 2]
                        kb.mms([(p[:, 0:T],
                                 w2b[:, f * D + m * 128:f * D + (m + 1) * 128],
                                 y_b[:, f * T:(f + 1) * T], f == 0, f == FC - 1)
                                for f in range(FC)])
                        kb.op("DVE", lambda e, p=p, m=m: e.scalar_tensor_tensor(
                            out=x[:, m * T:(m + 1) * T], in0=p[:, 0:T],
                            scalar=b2_c[:, m:m + 1], in1=x[:, m * T:(m + 1) * T],
                            op0=ALU.add, op1=ALU.add),
                            [p[:], b2_c[:], x[:]], [x[:]])

                load(0)
                p1a(0)
                p1b(0)
                for t in range(nT):
                    if t + 1 < nT:
                        load(t + 1)
                    p2y(t)
                    if t + 1 < nT:
                        p1a(t + 1)
                    p2z(t, range(0, 4))
                    if t + 1 < nT:
                        p1b(t + 1)
                    p2z(t, range(4, NC))
                    sts[t % 2].dma(dstv[:, :, t * T:(t + 1) * T], c3(x_t[t % 2][:]))
                for s in sts:
                    s.wait_all("SP")

        esW = contextlib.ExitStack()
        with esW:
            w1b = esW.enter_context(nc.sbuf_tensor("w1b", [128, NC * F], BF16))
            w2b = esW.enter_context(nc.sbuf_tensor("w2b", [128, FC * D], BF16))
            with contextlib.ExitStack() as esD:
                sqi = [esD.enter_context(nc.sbuf_tensor("dq_i%d" % i, [128, F], I32))
                       for i in (0, 1)]
                scb = [esD.enter_context(nc.sbuf_tensor("dq_s%d" % i, [128, F], F32))
                       for i in (0, 1)]
                ldq = [kb.stream("ldq%d" % i) for i in (0, 1)]
                stw = kb.stream("stw")

                def dq_load(n, jobs):
                    qd, sd, c, width, *_ = jobs[n]
                    i = n % 2
                    ldq[i].dma(sqi[i][:, 0:width], qd[c * 128:(c + 1) * 128, :])
                    ldq[i].dma(scb[i][:, 0:width],
                               sd[c:c + 1, :].partition_broadcast(128))

                def dq_proc(n, jobs):
                    qd, sd, c, width, wout, wdram, quarter = jobs[n]
                    i = n % 2
                    ldq[i].wait_all("DVE")
                    if quarter:
                        kb.op("DVE", lambda e, i=i, width=width:
                              e.tensor_scalar_mul(
                                  out=scb[i][:, 0:width], in0=scb[i][:, 0:width],
                                  scalar1=0.25), [scb[i][:]], [scb[i][:]])
                    kb.op("DVE", lambda e, i=i, c=c, width=width:
                          e.scalar_tensor_tensor(
                              out=wout[:, c * width:(c + 1) * width],
                              in0=sqi[i][:, 0:width], scalar=128.0,
                              in1=scb[i][:, 0:width],
                              op0=ALU.subtract, op1=ALU.mult),
                          [sqi[i][:], scb[i][:]], [wout[:]])
                    stw.dma(wdram[:, c * width:(c + 1) * width],
                            wout[:, c * width:(c + 1) * width])

                jobs = ([(fc1q, fc1s, c, F, w1b, w1d, False) for c in range(NC)]
                        + [(fc2q, fc2s, c, D, w2b, w2d, True)
                           for c in range(FC)])
                dq_load(0, jobs)
                for n in range(len(jobs)):
                    if n + 1 < len(jobs):
                        dq_load(n + 1, jobs)
                    dq_proc(n, jobs)
                stw.wait_all("SP")
            ffn_phase(xT, x1d, w1b, w2b, "A")
        kb.barrier()

        esKT = contextlib.ExitStack()
        with esKT:
            kTs = esKT.enter_context(nc.sbuf_tensor("kTs", [128, NC * S], BF16))
            esB = contextlib.ExitStack()
            with esB:
                sbB = lambda n, sh, dt: esB.enter_context(nc.sbuf_tensor(n, sh, dt))
                psB = lambda n, sh: esB.enter_context(nc.psum_tensor(n, sh, F32))
                Wqb = sbB("Wqb", [128, NC * D], BF16)
                Wkb = sbB("Wkb", [128, NC * D], BF16)
                Wvb = sbB("Wvb", [128, NC * D], BF16)
                bvb = sbB("bvb", [128, D], F32)
                wtmp = [sbB("wtmp%d" % i, [128, D], F32) for i in (0, 1)]
                x_t = [sbB("bx_t%d" % i, [128, NC * T], F32) for i in (0, 1)]
                u2_t = sbB("bu2", [128, NC * T], F32)
                u2b = [sbB("bu2b%d" % i, [128, NC * T], BF16) for i in (0, 1)]
                xb = sbB("bxb", [128, NC * T], BF16)
                sq = sbB("bsq", [128, NC * T], BF16)
                q_b = sbB("bq_b", [128, NC * T], BF16)
                v_sb = sbB("bv_sb", [128, 1040], BF16)
                qsq = sbB("bqsq", [128, NC * T], BF16)
                ksq = sbB("bksq", [128, NC * T], BF16)
                n2m = sbB("bn2m", [16, 1], F32)
                pstat = [psB("bpstat%d" % i, [128, 512]) for i in (0, 1)]
                pp = [psB("bpp%d" % i, [128, 512]) for i in (0, 1)]
                pv = [psB("bpv%d" % i, [128, 512]) for i in (0, 1)]
                pqn = psB("bpqn", [16, 512])
                pkn = psB("bpkn", [16, 512])
                ldw = kb.stream("ldwB")
                lds = [kb.stream("ldb%d" % i) for i in (0, 1)]
                sts = [kb.stream("stb%d" % i) for i in (0, 1)]

                wjobs = [(Wd, Wsb, c) for Wd, Wsb in
                         ((Wq, Wqb), (Wk, Wkb), (Wv, Wvb)) for c in range(NC)]
                ldww = [kb.stream("ldww%d" % i) for i in (0, 1)]
                ldww[0].dma(wtmp[0][:], wjobs[0][0][0:128, :])
                for n, (Wd, Wsb, c) in enumerate(wjobs):
                    if n + 1 < len(wjobs):
                        W2, _, c2 = wjobs[n + 1]
                        ldww[(n + 1) % 2].dma(wtmp[(n + 1) % 2][:],
                                              W2[c2 * 128:(c2 + 1) * 128, :])
                    ldww[n % 2].wait_all("DVE")
                    w = wtmp[n % 2]
                    kb.op("DVE", lambda e, w=w, Wsb=Wsb, c=c: e.tensor_copy(
                        out=Wsb[:, c * D:(c + 1) * D], in_=w[:]),
                        [w[:]], [Wsb[:]])
                ldw.dma(bvb[:], bv.rearrange("(o d) -> o d", o=1)
                        .partition_broadcast(128))

                kb.op("DVE", lambda e: e.memset(v_sb[:], 1.0), [], [v_sb[:]])
                x1v, u2v, qv = fm(x1d), fm(u2d), fm(qTd)
                kv3 = kTs.ap().rearrange("p (c s) -> p c s", s=S)

                def loadB(t):
                    lds[t % 2].dma(c3(x_t[t % 2][:]), x1v[:, :, t * T:(t + 1) * T])

                def bp1cast(t):
                    x = x_t[t % 2]
                    lds[t % 2].wait_all("DVE")
                    kb.op("DVE", lambda e, x=x: e.tensor_copy(out=xb[:], in_=x[:]),
                          [x[:]], [xb[:]])
                    kb.op("DVE", lambda e: e.tensor_mul(out=sq[:], in0=xb[:],
                                                        in1=xb[:]),
                          [xb[:]], [sq[:]])

                def bp1(t):
                    x = x_t[t % 2]
                    ub = u2b[t % 2]
                    mean, rstd = ln_stats_pre(xb, sq, pstat[0], 1)
                    kb.op("DVE", lambda e: e.tensor_tensor(
                        out=c3(u2_t[:]), in0=c3(x[:]), in1=bc(mean[:]),
                        op=ALU.subtract), [x[:], mean[:]], [u2_t[:]])
                    kb.op("DVE", lambda e, ub=ub: e.tensor_tensor(
                        out=c3(ub[:]), in0=c3(u2_t[:]), in1=bc(rstd[:]),
                        op=ALU.mult), [u2_t[:], rstd[:]], [ub[:]])
                    sts[t % 2].dma(u2v[:, :, t * T:(t + 1) * T], c3(ub[:]))

                def bp2q(t):
                    ub = u2b[t % 2]
                    for m in range(NC):
                        p = pp[m % 2]
                        kb.mms([(p[:, 0:T],
                                 Wqb[:, c * D + m * 128:c * D + (m + 1) * 128],
                                 ub[:, c * T:(c + 1) * T], c == 0, c == NC - 1)
                                for c in range(NC)])
                        kb.op("ACT", lambda e, p=p, m=m: e.activation(
                            out=q_b[:, m * T:(m + 1) * T], in_=p[:, 0:T],
                            func=AF.Identity, bias=bq_c[:, m:m + 1], scale=1.0),
                            [p[:], bq_c[:]], [q_b[:]])
                    sts[t % 2].dma(qv[:, :, t * T:(t + 1) * T], c3(q_b[:]))
                    kb.op("DVE", lambda e: e.tensor_mul(
                        out=qsq[:], in0=q_b[:], in1=q_b[:]), [q_b[:]], [qsq[:]])

                def bp2k(t):
                    ub = u2b[t % 2]
                    for m in range(NC):
                        p = pp[m % 2]
                        kb.mms([(p[:, 0:T],
                                 Wkb[:, c * D + m * 128:c * D + (m + 1) * 128],
                                 ub[:, c * T:(c + 1) * T], c == 0, c == NC - 1)
                                for c in range(NC)])
                        kb.op("ACT", lambda e, p=p, m=m, t=t: e.activation(
                            out=kTs[:, m * S + t * T:m * S + (t + 1) * T],
                            in_=p[:, 0:T], func=AF.Identity,
                            bias=bk_c[:, m:m + 1], scale=1.0),
                            [p[:], bk_c[:]], [kTs[:]])
                    kb.op("DVE", lambda e, t=t: e.tensor_mul(
                        out=c3(ksq[:]), in0=kv3[:, :, t * T:(t + 1) * T],
                        in1=kv3[:, :, t * T:(t + 1) * T]), [kTs[:]], [ksq[:]])

                def bp2v(t):
                    ub = u2b[t % 2]
                    for sub in range(T // 128):
                        for dvc in range(2):
                            p = pv[dvc]
                            kb.mms([(p[:, 0:512],
                                     ub[:, c * T + sub * 128:c * T + (sub + 1) * 128],
                                     Wvb[:, c * D + dvc * 512:c * D + (dvc + 1) * 512],
                                     c == 0, c == NC - 1) for c in range(NC)])
                            vout = v_sb.ap()[:, dvc * 520:dvc * 520 + 520] \
                                .rearrange("p (h c) -> p h c", c=65)[:, 0:8, 0:64]
                            kb.op("DVE", lambda e, p=p, dvc=dvc, vout=vout:
                                  e.tensor_tensor(
                                      out=vout,
                                      in0=p.ap()[:, 0:512].rearrange(
                                          "p (h c) -> p h c", c=64),
                                      in1=bvb.ap()[:, dvc * 512:(dvc + 1) * 512]
                                      .rearrange("p (h c) -> p h c", c=64),
                                      op=ALU.add),
                                  [p[:], bvb[:]], [v_sb[:]])
                        row0 = t * T + sub * 128
                        sts[t % 2].dma(vd[row0:row0 + 128, :], v_sb[:])

                def bp2sel(t):
                    for srcb, prun, run in ((qsq, pqn, qn2run), (ksq, pkn, kn2run)):
                        kb.mms([(prun[0:16, 0:T], sel_sb[:, c * 16:(c + 1) * 16],
                                 srcb[:, c * T:(c + 1) * T], c == 0, c == NC - 1)
                                for c in range(NC)])
                        kb.op("DVE", lambda e, prun=prun: e.reduce_max(
                            out=n2m[:], in_=prun[0:16, 0:T], axis=AX.X),
                            [prun[:]], [n2m[:]])
                        if t == 0:
                            kb.op("DVE", lambda e, run=run: e.tensor_copy(
                                out=run[:], in_=n2m[:]), [n2m[:]], [run[:]])
                        else:
                            kb.op("DVE", lambda e, run=run: e.tensor_tensor(
                                out=run[:], in0=run[:], in1=n2m[:], op=ALU.max),
                                [run[:], n2m[:]], [run[:]])

                loadB(0)
                bp1cast(0)
                bp1(0)
                for t in range(nT):
                    if t + 1 < nT:
                        loadB(t + 1)
                    bp2q(t)
                    if t + 1 < nT:
                        bp1cast(t + 1)
                    bp2k(t)
                    if t + 1 < nT:
                        bp1(t + 1)
                    bp2v(t)
                    bp2sel(t)
                kb.op("DVE", lambda e: e.tensor_mul(out=qn2run[:], in0=qn2run[:],
                                                    in1=kn2run[:]),
                      [qn2run[:], kn2run[:]], [qn2run[:]])
                kb.op("ACT", lambda e: e.activation(out=qn2run[:], in_=qn2run[:],
                                                    func=AF.Sqrt, bias=0.0,
                                                    scale=1.0 / 64.0),
                      [qn2run[:]], [qn2run[:]])
                kb.op("PE", lambda e: e.transpose(out=pstat[0][0:1, 0:16],
                                                  in_=qn2run[:],
                                                  identity=id16_sb[:]),
                      [qn2run[:], id16_sb[:]], [pstat[0][:]])
                kb.op("DVE", lambda e: e.tensor_scalar_mul(
                    out=negbrow[:], in0=pstat[0][0:1, 0:16], scalar1=-1.0),
                    [pstat[0][:]], [negbrow[:]])
                kb.op("DVE", lambda e: e.tensor_tensor(
                    out=negbrow[:, 0:8], in0=negbrow[0:1, 0:16:2],
                    in1=negbrow[0:1, 1:16:2], op=ALU.min),
                    [negbrow[:]], [negbrow[:]])
                kb.mms([(pstat[1][0:128, 0:8], ones_f[:], negbrow[:, 0:8],
                         True, True)])
                kb.op("DVE", lambda e: e.tensor_copy(out=negb_c[:],
                                                     in_=pstat[1][0:128, 0:8]),
                      [pstat[1][:]], [negb_c[:]])
                for s in sts:
                    s.wait_all("SP")
            kb.barrier()

            esT = contextlib.ExitStack()
            with esT:
                sbT = lambda n, sh, dt: esT.enter_context(nc.sbuf_tensor(n, sh, dt))
                psT = lambda n, sh: esT.enter_context(nc.psum_tensor(n, sh, F32))
                v65 = sbT("v65", [128, nKB * 1040], BF16)
                qTg = [sbT("qTg%d" % i, [128, NC * QG], BF16) for i in (0, 1)]
                PT = [sbT("PT%d" % i, [128, 1024], BF16) for i in (0, 1)]
                oTg = [sbT("oTg%d" % i, [128, NC * QG], BF16) for i in (0, 1)]
                ssb = [sbT("ssb%d" % i, [65, 1024], F32) for i in (0, 1)]
                pS = [psT("pS%d" % i, [128, 1024]) for i in (0, 1)]
                pAV = [psT("pAV%d" % i, [128, 512]) for i in range(4)]
                ldv = kb.stream("ldv")
                ldqg = [kb.stream("ldqg%d" % i) for i in (0, 1)]
                sto = kb.stream("sto")

                vsrc = vd.rearrange("(b p) c -> p b c", p=128)
                vdst = v65.ap().rearrange("p (b c) -> p b c", c=1040)
                for j in range(4):
                    blk = nKB // 4
                    ldv.dma(vdst[:, j * blk:(j + 1) * blk, :],
                            vsrc[:, j * blk:(j + 1) * blk, :])
                qgv = fm(qTd)

                def loadG(g):
                    ldqg[g % 2].dma(
                        qTg[g % 2].ap().rearrange("p (c s) -> p c s", s=QG),
                        qgv[:, :, g * QG:(g + 1) * QG])

                loadG(0)
                for g in range(nG):
                    if g + 1 < nG:
                        loadG(g + 1)
                    qg = qTg[g % 2]
                    ldqg[g % 2].wait_all("PE")
                    ldv.wait_all("PE")
                    kbmax = 4 * g + 3
                    units = [(c, kbi) for c in range(8)
                             for kbi in range(kbmax + 1)]

                    def emit_S(idx):
                        c, kbi = units[idx]
                        p = pS[idx % 2]
                        lA = kTs[0:64, c * S + kbi * 128:c * S + (kbi + 1) * 128]
                        lB = kTs[64:128, c * S + kbi * 128:c * S + (kbi + 1) * 128]
                        rA = qg[0:64, c * QG:(c + 1) * QG]
                        rB = qg[64:128, c * QG:(c + 1) * QG]
                        m = kbi - 4 * g
                        items = []
                        for off, l, r in ((0, lA, rA), (512, lB, rB)):
                            if 0 <= m:
                                items += [(p[:, off:off + 512], l, r,
                                           True, False),
                                          (p[:, off + m * 128:
                                             off + (m + 1) * 128],
                                           id128_sb[:], mask_sb[:],
                                           False, True)]
                            else:
                                items.append((p[:, off:off + 512], l, r,
                                              True, True))
                        kb.mms(items)

                    def emit_exp(idx):
                        c, kbi = units[idx]
                        p = pS[idx % 2]
                        pt = PT[idx % 2]
                        kb.op("ACT", lambda e, p=p, pt=pt, c=c: e.activation(
                            out=pt[:], in_=p[:], func=AF.Exp,
                            bias=negb_c[:, c:c + 1], scale=0.125),
                            [p[:], negb_c[:]], [pt[:]])

                    def emit_AV(idx):
                        c, kbi = units[idx]
                        pt = PT[idx % 2]
                        pavA, pavB = pAV[2 * (c % 2)], pAV[2 * (c % 2) + 1]
                        hA, hB = 2 * c, 2 * c + 1
                        m = max(kbi - 4 * g, 0)
                        q0 = m * 128
                        vA = v65[:, kbi * 1040 + hA * 65:
                                 kbi * 1040 + hA * 65 + 65]
                        vB = v65[:, kbi * 1040 + hB * 65:
                                 kbi * 1040 + hB * 65 + 65]
                        kb.mms([(pavA[0:65, q0:QG], vA, pt[:, q0:512],
                                 kbi == 0, kbi == kbmax),
                                (pavB[0:65, q0:QG], vB, pt[:, 512 + q0:1024],
                                 kbi == 0, kbi == kbmax)])

                    def emit_tail(c):
                        og = oTg[g % 2]
                        sb_ = ssb[c % 2]
                        pavA, pavB = pAV[2 * (c % 2)], pAV[2 * (c % 2) + 1]
                        for hh, pav in ((0, pavA), (1, pavB)):
                            rows = slice(0, 64) if hh == 0 else slice(64, 128)
                            kb.op("DVE", lambda e, pav=pav, rows=rows, c=c,
                                  og=og: e.tensor_copy(
                                      out=og[rows, c * QG:(c + 1) * QG],
                                      in_=pav[0:64, 0:QG]),
                                  [pav[:]], [og[:]])
                            kb.op("DVE", lambda e, pav=pav, sb_=sb_, hh=hh:
                                  e.tensor_copy(
                                      out=sb_[64:65, hh * QG:(hh + 1) * QG],
                                      in_=pav[64:65, 0:QG]),
                                  [pav[:]], [sb_[:]])
                            sto.dma(sumd[2 * c + hh:2 * c + hh + 1,
                                         g * QG:(g + 1) * QG],
                                    sb_[64:65, hh * QG:(hh + 1) * QG])

                    emit_S(0)
                    for idx in range(len(units)):
                        emit_exp(idx)
                        if idx + 1 < len(units):
                            emit_S(idx + 1)
                        emit_AV(idx)
                        c, kbi = units[idx]
                        if kbi == kbmax:
                            emit_tail(c)
                    sto.dma(fm(oTd)[:, :, g * QG:(g + 1) * QG],
                            oTg[g % 2].ap().rearrange("p (c s) -> p c s", s=QG))
                sto.wait_all("SP")
        kb.barrier()

        esE = contextlib.ExitStack()
        with esE:
            sbE = lambda n, sh, dt: esE.enter_context(nc.sbuf_tensor(n, sh, dt))
            psE = lambda n, sh: esE.enter_context(nc.psum_tensor(n, sh, F32))
            Wob = sbE("Wob", [128, NC * D], BF16)
            wtmp = [sbE("ewtmp%d" % i, [128, D], F32) for i in (0, 1)]
            o_t = [sbE("eo_t%d" % i, [128, NC * T], BF16) for i in (0, 1)]
            u2_t = [sbE("eu2%d" % i, [128, NC * T], BF16) for i in (0, 1)]
            x1_t = [sbE("ex1%d" % i, [128, NC * T], F32) for i in (0, 1)]
            y_t = [sbE("ey_t%d" % i, [128, NC * T], F32) for i in (0, 1)]
            xb = sbE("exb", [128, NC * T], BF16)
            sq = sbE("esq", [128, NC * T], BF16)
            ssb2 = sbE("essum", [16, S], F32)
            rr_b = sbE("err_b", [16, S], BF16)
            pstat = [psE("epstat%d" % i, [128, 512]) for i in (0, 1)]
            po = [psE("epo%d" % i, [128, 512]) for i in (0, 1)]
            pR = [psE("epR%d" % i, [128, 512]) for i in (0, 1)]
            ldw = kb.stream("ldwE")
            lds = [kb.stream("lde%d" % i) for i in (0, 1)]
            sts = [kb.stream("ste%d" % i) for i in (0, 1)]

            ldsm = kb.stream("ldsm")
            ldsm.dma(ssb2[:], sumd)
            ldsm.wait_all("DVE")
            kb.op("DVE", lambda e: e.reciprocal(out=ssb2[:], in_=ssb2[:]),
                  [ssb2[:]], [ssb2[:]])
            kb.op("DVE", lambda e: e.tensor_copy(out=rr_b[:], in_=ssb2[:]),
                  [ssb2[:]], [rr_b[:]])
            ldwe = [kb.stream("ldwe%d" % i) for i in (0, 1)]
            ldwe[0].dma(wtmp[0][:], Wo[0:128, :])
            for c in range(NC):
                if c + 1 < NC:
                    ldwe[(c + 1) % 2].dma(wtmp[(c + 1) % 2][:],
                                          Wo[(c + 1) * 128:(c + 2) * 128, :])
                ldwe[c % 2].wait_all("DVE")
                w = wtmp[c % 2]
                kb.op("DVE", lambda e, w=w, c=c: e.tensor_copy(
                    out=Wob[:, c * D:(c + 1) * D], in_=w[:]), [w[:]], [Wob[:]])
            ov, u2v, x1v, x2v = fm(oTd), fm(u2d), fm(x1d), fm(x2d)

            def loadE(t):
                i = t % 2
                lds[i].dma(c3(o_t[i][:]), ov[:, :, t * T:(t + 1) * T])
                lds[i].dma(c3(u2_t[i][:]), u2v[:, :, t * T:(t + 1) * T])
                lds[i].dma(c3(x1_t[i][:]), x1v[:, :, t * T:(t + 1) * T])


            def ep1(t):
                i = t % 2
                lds[i].wait_all("PE")
                lds[i].wait_all("DVE")
                for m in range(NC):
                    pr = pR[m % 2]
                    kb.mms([(pr[:, 0:T], selh_sb[:, m * 128:(m + 1) * 128],
                             rr_b[0:16, t * T:(t + 1) * T], True, True)])
                    kb.op("DVE", lambda e, i=i, m=m, pr=pr: e.tensor_mul(
                        out=o_t[i][:, m * T:(m + 1) * T],
                        in0=o_t[i][:, m * T:(m + 1) * T], in1=pr[:, 0:T]),
                        [o_t[i][:], pr[:]], [o_t[i][:]])
                for m in range(NC):
                    p = po[m % 2]
                    kb.mms([(p[:, 0:T],
                             Wob[:, c * D + m * 128:c * D + (m + 1) * 128],
                             o_t[i][:, c * T:(c + 1) * T], c == 0, c == NC - 1)
                            for c in range(NC)])
                    kb.op("DVE", lambda e, p=p, m=m, i=i: e.scalar_tensor_tensor(
                        out=y_t[i][:, m * T:(m + 1) * T], in0=p[:, 0:T],
                        scalar=bo_c[:, m:m + 1], in1=u2_t[i][:, m * T:(m + 1) * T],
                        op0=ALU.add, op1=ALU.add),
                        [p[:], bo_c[:], u2_t[i][:]], [y_t[i][:]])

            def ep2(t):
                i = t % 2
                y, x1 = y_t[i], x1_t[i]
                mean, rstd = ln_stats(y[:], xb, sq, pstat[0], 1)
                kb.op("POOL", lambda e, y=y: e.tensor_tensor(
                    out=c3(y[:]), in0=c3(y[:]), in1=bc(mean[:]),
                    op=ALU.subtract), [y[:], mean[:]], [y[:]])
                kb.op("POOL", lambda e, y=y: e.tensor_tensor(
                    out=c3(y[:]), in0=c3(y[:]), in1=bc(rstd[:]),
                    op=ALU.mult), [y[:], rstd[:]], [y[:]])
                for m in range(NC):
                    ms = slice(m * T, (m + 1) * T)
                    kb.op("DVE", lambda e, m=m, ms=ms, x1=x1, y=y:
                          e.scalar_tensor_tensor(
                              out=x1[:, ms], in0=y[:, ms],
                              scalar=agam_c[:, m:m + 1], in1=x1[:, ms],
                              op0=ALU.mult, op1=ALU.add),
                          [y[:], agam_c[:], x1[:]], [x1[:]])
                    kb.op("DVE", lambda e, m=m, ms=ms, x1=x1: e.tensor_scalar(
                        out=x1[:, ms], in0=x1[:, ms], scalar1=abet_c[:, m:m + 1],
                        scalar2=None, op0=ALU.add),
                        [x1[:], abet_c[:]], [x1[:]])
                sts[i].dma(x2v[:, :, t * T:(t + 1) * T], c3(x1[:]))

            loadE(0)
            ep1(0)
            for t in range(nT):
                if t + 1 < nT:
                    loadE(t + 1)
                ep2(t)
                if t + 1 < nT:
                    ep1(t + 1)
            for s in sts:
                s.wait_all("SP")
        kb.barrier()

        esC = contextlib.ExitStack()
        with esC:
            w1b = esC.enter_context(nc.sbuf_tensor("w1bC", [128, NC * F], BF16))
            w2b = esC.enter_context(nc.sbuf_tensor("w2bC", [128, FC * D], BF16))
            ldw = kb.stream("ldwC")
            for j in range(4):
                w = NC * F // 4
                ldw.dma(w1b[:, j * w:(j + 1) * w], w1d[:, j * w:(j + 1) * w])
                ldw.dma(w2b[:, j * w:(j + 1) * w], w2d[:, j * w:(j + 1) * w])
            ldw.wait_all("PE")
            ffn_phase(x2d, outT, w1b, w2b, "C")
    return nc


def kernel(**inputs):
    S = 4096
    x = np.asarray(inputs["L0_attn_x"], dtype=np.float32)
    nm = {
        "Wq": "L0_attn_Wq", "bq": "L0_attn_bq", "Wk": "L0_attn_Wk",
        "bk": "L0_attn_bk", "Wv": "L0_attn_Wv", "bv": "L0_attn_bv",
        "Wo": "L0_attn_Wo", "bo": "L0_attn_bo", "agam": "L0_attn_gamma",
        "abet": "L0_attn_beta", "lgam": "L0_ffn_ln_gamma",
        "lbet": "L0_ffn_ln_beta", "fc1s": "L0_ffn_fc1_scales",
        "fc1b": "L0_ffn_fc1_bias", "fc2s": "L0_ffn_fc2_scales",
        "fc2b": "L0_ffn_fc2_bias",
    }
    base = {k: np.ascontiguousarray(np.asarray(inputs[v], dtype=np.float32))
            for k, v in nm.items()}
    base["fc1q"] = np.ascontiguousarray(np.asarray(inputs["L0_ffn_fc1_q"],
                                                   dtype=np.int32))
    base["fc2q"] = np.ascontiguousarray(np.asarray(inputs["L0_ffn_fc2_q"],
                                                   dtype=np.int32))
    in_maps = []
    for b in range(8):
        m = dict(base)
        m["xT"] = np.ascontiguousarray(x[b].T)
        in_maps.append(m)
    nc = build_nc(S)
    res = run_bass_kernel_spmd(nc, in_maps, core_ids=list(range(8)))
    out = np.stack([np.asarray(r["outT"]).T for r in res.results], axis=0)
    return out.astype(np.float32)


# version 14
# speedup vs baseline: 1.0684x; 1.0449x over previous
import contextlib
import numpy as np

try:
    import concourse.bass as bass
except ImportError:
    import sys
    sys.path.insert(0, "/opt/trn_rl_repo")
    import concourse.bass as bass
import concourse.mybir as mybir
from concourse.bass_utils import run_bass_kernel_spmd

F32 = mybir.dt.float32
BF16 = mybir.dt.bfloat16
I32 = mybir.dt.int32
AF = mybir.ActivationFunctionType
ALU = mybir.AluOpType
AX = mybir.AxisListType

D = 1024
F = 4096
H = 16
HD = 64
NC = 8
FC = 32
T = 256
QG = 512
EPS = 1e-5
NEG = -1.0e30


class KB:

    def __init__(self, nc, es):
        self.nc = nc
        self.es = es
        self.eng = dict(PE=nc.tensor, ACT=nc.scalar, DVE=nc.vector,
                        POOL=nc.gpsimd, SP=nc.sync)
        self.sem = {k: es.enter_context(nc.semaphore("e_" + k)) for k in self.eng}
        self.cnt = {k: 0 for k in self.eng}
        self.wr = {}
        self.rd = {}
        self.waited = {}
        self.semobj = {k: self.sem[k] for k in self.eng}
        self.streams = {}

    def _wait(self, eng, semkey, val):
        if semkey == eng:
            return
        if semkey in self.streams:
            val = 16 * self.streams[semkey].issued
        else:
            assert val is not None and val <= self.cnt[semkey], (semkey, val)
        w = (eng, semkey)
        if self.waited.get(w, -1) >= val:
            return
        self.eng[eng].wait_ge(self.semobj[semkey], val)
        self.waited[w] = val

    def dep(self, eng, rkeys, wkeys):
        for k in rkeys:
            if k in self.wr:
                s, v = self.wr[k]
                self._wait(eng, s, v)
        for k in wkeys:
            if k in self.wr:
                s, v = self.wr[k]
                self._wait(eng, s, v)
            for s, v in self.rd.get(k, {}).items():
                self._wait(eng, s, v)

    def _note(self, eng_semkey, val, rkeys, wkeys):
        for k in wkeys:
            self.wr[k] = (eng_semkey, val)
            self.rd[k] = {}
        for k in rkeys:
            self.rd.setdefault(k, {})[eng_semkey] = val

    @staticmethod
    def keys(aps):
        return [a.name for a in aps]

    def op(self, eng, fn, reads, writes):
        rk, wk = self.keys(reads), self.keys(writes)
        self.dep(eng, rk, wk)
        ins = fn(self.eng[eng])
        self.cnt[eng] += 1
        ins.then_inc(self.sem[eng], 1)
        self._note(eng, self.cnt[eng], rk, wk)
        return ins

    def mms(self, items):
        rk, wk = [], []
        for o, l, r, st, sp in items:
            rk += self.keys([l, r])
            wk.append(o.name)
        self.dep("PE", rk, wk)
        for o, l, r, st, sp in items:
            ins = self.eng["PE"].matmul(out=o, lhsT=l, rhs=r, start=st, stop=sp)
        self.cnt["PE"] += 1
        ins.then_inc(self.sem["PE"], 1)
        self._note("PE", self.cnt["PE"], rk, wk)

    def barrier(self):
        self.nc.all_engine_barrier()
        self.wr = {k: v for k, v in self.wr.items() if v[0] in self.streams}
        self.rd = {k: {s: v for s, v in d.items() if s in self.streams}
                   for k, d in self.rd.items()}
        self.rd = {k: d for k, d in self.rd.items() if d}
        self.waited = {}

    def stream(self, name):
        s = DmaStream(self, name, self.es)
        self.streams[name] = s
        self.semobj[name] = s.sem
        return s


class DmaStream:
    def __init__(self, kb, name, es):
        self.kb = kb
        self.name = name
        self.sem = es.enter_context(kb.nc.semaphore("dma_" + name))
        self.issued = 0

    def dma(self, out, in_, **kw):
        kb = self.kb
        rk, wk = [in_.name], [out.name]
        kb.dep("SP", rk, wk)
        ins = kb.eng["SP"].dma_start(out=out, in_=in_, **kw)
        ins.then_inc(self.sem, 16)
        self.issued += 1
        kb._note(self.name, None, rk, wk)

    def wait_all(self, eng):
        self.kb._wait(eng, self.name, None)


def build_nc(S):
    nT = S // T
    nG = S // QG
    nKB = S // 128
    nc = bass.Bass("TRN2", target_bir_lowering=False,
                   detect_race_conditions=False)

    dt_in = lambda n, sh, dt: nc.dram_tensor(n, sh, dt, kind="ExternalInput").ap()
    dt_int = lambda n, sh, dt: nc.dram_tensor(n, sh, dt, kind="Internal").ap()

    xT = dt_in("xT", [D, S], F32)
    Wq, Wk, Wv, Wo = (dt_in(n, [D, D], F32) for n in ("Wq", "Wk", "Wv", "Wo"))
    bq, bk, bv, bo = (dt_in(n, [D], F32) for n in ("bq", "bk", "bv", "bo"))
    agam, abet = dt_in("agam", [D], F32), dt_in("abet", [D], F32)
    lgam, lbet = dt_in("lgam", [D], F32), dt_in("lbet", [D], F32)
    fc1q = dt_in("fc1q", [D, F], I32)
    fc1s = dt_in("fc1s", [NC, F], F32)
    fc1b = dt_in("fc1b", [F], F32)
    fc2q = dt_in("fc2q", [F, D], I32)
    fc2s = dt_in("fc2s", [FC, D], F32)
    fc2b = dt_in("fc2b", [D], F32)
    outT = nc.dram_tensor("outT", [D, S], F32, kind="ExternalOutput").ap()

    x1d = dt_int("x1d", [D, S], F32)
    u2d = dt_int("u2d", [D, S], BF16)
    x2d = dt_int("x2d", [D, S], F32)
    qTd = dt_int("qTd", [D, S], BF16)
    vd = dt_int("vd", [S, 1040], BF16)
    oTd = dt_int("oTd", [D, S], BF16)
    sumd = dt_int("sumd", [H, S], F32)
    w1d = dt_int("w1d", [128, NC * F], BF16)
    w2d = dt_int("w2d", [128, FC * D], BF16)

    import ml_dtypes
    mask_np = np.where(np.arange(128)[:, None] > np.arange(128)[None, :],
                       np.float32(NEG), np.float32(0.0))
    maskc = nc.inline_tensor(mask_np.astype(ml_dtypes.bfloat16), name="maskc").ap()
    id128c = nc.inline_tensor(np.eye(128, dtype=np.float32)
                              .astype(ml_dtypes.bfloat16), name="id128c").ap()
    sel_np = np.zeros((128, NC * 16), dtype=np.float32)
    for c in range(NC):
        sel_np[0:64, c * 16 + 2 * c] = 1.0
        sel_np[64:128, c * 16 + 2 * c + 1] = 1.0
    selc = nc.inline_tensor(sel_np.astype(ml_dtypes.bfloat16), name="selc").ap()
    selh_np = np.zeros((16, NC * 128), dtype=np.float32)
    for c in range(NC):
        selh_np[2 * c, c * 128:c * 128 + 64] = 1.0
        selh_np[2 * c + 1, c * 128 + 64:(c + 1) * 128] = 1.0
    selhc = nc.inline_tensor(selh_np.astype(ml_dtypes.bfloat16), name="selhc").ap()
    id16 = nc.inline_tensor(np.eye(16, dtype=np.float32), name="id16").ap()

    def fm(dram):
        return dram.rearrange("(c p) s -> p c s", p=128)

    es0 = contextlib.ExitStack()
    with es0:
        kb = KB(nc, es0)
        sb = lambda n, sh, dt: es0.enter_context(nc.sbuf_tensor(n, sh, dt))

        mask_sb = sb("mask_sb", [128, 128], BF16)
        id128_sb = sb("id128_sb", [128, 128], BF16)
        sel_sb = sb("sel_sb", [128, NC * 16], BF16)
        id16_sb = sb("id16_sb", [16, 16], F32)
        selh_sb = sb("selh_sb", [16, NC * 128], BF16)
        lgam_c = sb("lgam_c", [128, NC], F32)
        lbet_c = sb("lbet_c", [128, NC], F32)
        agam_c = sb("agam_c", [128, NC], F32)
        abet_c = sb("abet_c", [128, NC], F32)
        bq_c = sb("bq_c", [128, NC], F32)
        bk_c = sb("bk_c", [128, NC], F32)
        bo_c = sb("bo_c", [128, NC], F32)
        b1_c = sb("b1_c", [128, FC], F32)
        b2_c = sb("b2_c", [128, NC], F32)
        onesP = sb("onesP", [128, 128], BF16)
        eps_c = sb("eps_c", [128, 1], F32)
        ones_f = sb("ones_f", [1, 128], F32)
        quart_f = sb("quart_f", [1, 128], F32)
        ones_b = sb("ones_b", [1, 128], BF16)
        qn2run = sb("qn2run", [16, 1], F32)
        kn2run = sb("kn2run", [16, 1], F32)
        negbrow = sb("negbrow", [1, 16], F32)
        negb_c = sb("negb_c", [128, 8], F32)
        stat = {n: sb("st_" + n, [128, T], F32)
                for n in ("mean", "v", "r", "mean2", "v2", "r2")}

        ld0 = kb.stream("ld0")
        colv = lambda v: v.rearrange("(m p) -> p m", p=128)
        for dst, src in ((lgam_c, lgam), (lbet_c, lbet), (agam_c, agam),
                         (abet_c, abet), (bq_c, bq), (bk_c, bk), (bo_c, bo),
                         (b1_c, fc1b), (b2_c, fc2b)):
            ld0.dma(dst[:], colv(src), allow_slow_non_contiguous=True)
        ld0.dma(mask_sb[:], maskc)
        ld0.dma(id128_sb[:], id128c)
        ld0.dma(sel_sb[:], selc)
        ld0.dma(selh_sb[:], selhc)
        ld0.dma(id16_sb[:], id16)
        kb.op("DVE", lambda e: e.memset(onesP[:], 1.0 / 1024.0), [], [onesP[:]])
        kb.op("DVE", lambda e: e.memset(eps_c[:], EPS), [], [eps_c[:]])
        kb.op("DVE", lambda e: e.memset(ones_f[:], 1.0), [], [ones_f[:]])
        kb.op("DVE", lambda e: e.memset(quart_f[:], 0.25), [], [quart_f[:]])
        kb.op("DVE", lambda e: e.memset(ones_b[:], 1.0), [], [ones_b[:]])
        ld0.wait_all("DVE")
        kb.op("DVE", lambda e: e.tensor_scalar_mul(out=b2_c[:], in0=b2_c[:],
                                                   scalar1=0.25),
              [b2_c[:]], [b2_c[:]])

        def ln_stats(src, xb, sq, pstat, sfx):
            mean, var, rstd = (stat[k + ("" if sfx == 1 else "2")]
                               for k in ("mean", "v", "r"))
            kb.op("DVE", lambda e: e.tensor_copy(out=xb[:], in_=src[:]),
                  [src[:]], [xb[:]])
            kb.op("DVE", lambda e: e.tensor_mul(out=sq[:], in0=xb[:], in1=xb[:]),
                  [xb[:]], [sq[:]])
            kb.mms([(pstat[:, 0:T], onesP[:], xb[:, c * T:(c + 1) * T],
                     c == 0, c == NC - 1) for c in range(NC)])
            kb.mms([(pstat[:, T:2 * T], onesP[:], sq[:, c * T:(c + 1) * T],
                     c == 0, c == NC - 1) for c in range(NC)])
            kb.op("DVE", lambda e: e.tensor_copy(out=mean[:], in_=pstat[:, 0:T]),
                  [pstat[:]], [mean[:]])
            kb.op("DVE", lambda e: e.tensor_mul(out=var[:], in0=mean[:], in1=mean[:]),
                  [mean[:]], [var[:]])
            kb.op("DVE", lambda e: e.tensor_tensor(out=var[:], in0=pstat[:, T:2 * T],
                                                   in1=var[:], op=ALU.subtract),
                  [pstat[:], var[:]], [var[:]])
            kb.op("ACT", lambda e: e.activation(out=rstd[:], in_=var[:], func=AF.Ln,
                                                bias=eps_c[:], scale=1.0),
                  [var[:], eps_c[:]], [rstd[:]])
            kb.op("ACT", lambda e: e.activation(out=rstd[:], in_=rstd[:],
                                                func=AF.Exp, bias=0.0, scale=-0.5),
                  [rstd[:]], [rstd[:]])
            return mean, rstd

        def ln_stats_pre(xb_t, sq, pstat, sfx):
            mean, var, rstd = (stat[k + ("" if sfx == 1 else "2")]
                               for k in ("mean", "v", "r"))
            kb.mms([(pstat[:, 0:T], onesP[:], xb_t[:, c * T:(c + 1) * T],
                     c == 0, c == NC - 1) for c in range(NC)])
            kb.mms([(pstat[:, T:2 * T], onesP[:], sq[:, c * T:(c + 1) * T],
                     c == 0, c == NC - 1) for c in range(NC)])
            kb.op("DVE", lambda e: e.tensor_copy(out=mean[:], in_=pstat[:, 0:T]),
                  [pstat[:]], [mean[:]])
            kb.op("DVE", lambda e: e.tensor_mul(out=var[:], in0=mean[:], in1=mean[:]),
                  [mean[:]], [var[:]])
            kb.op("DVE", lambda e: e.tensor_tensor(out=var[:], in0=pstat[:, T:2 * T],
                                                   in1=var[:], op=ALU.subtract),
                  [pstat[:], var[:]], [var[:]])
            kb.op("ACT", lambda e: e.activation(out=rstd[:], in_=var[:], func=AF.Ln,
                                                bias=eps_c[:], scale=1.0),
                  [var[:], eps_c[:]], [rstd[:]])
            kb.op("ACT", lambda e: e.activation(out=rstd[:], in_=rstd[:],
                                                func=AF.Exp, bias=0.0, scale=-0.5),
                  [rstd[:]], [rstd[:]])
            return mean, rstd

        def bc(ap):
            return ap.rearrange("p (o s) -> p o s", o=1).broadcast_to([128, NC, T])

        def c3(ap):
            return ap.rearrange("p (c s) -> p c s", s=T)

        def ffn_phase(src_dram, dst_dram, w1b, w2b, ph):
            esA = contextlib.ExitStack()
            with esA:
                sbA = lambda n, sh, dt: esA.enter_context(
                    nc.sbuf_tensor(n + ph, sh, dt))
                psA = lambda n, sh: esA.enter_context(
                    nc.psum_tensor(n + ph, sh, F32))
                x_t = [sbA("x_t%d" % i, [128, NC * T], F32) for i in (0, 1)]
                u_t = sbA("u_t", [128, NC * T], BF16)
                hf = sbA("hf", [128, NC * T], F32)
                h_b = [sbA("h_b%d" % i, [128, NC * T], BF16) for i in (0, 1)]
                xb = sbA("xb", [128, NC * T], BF16)
                sq = sbA("sq", [128, NC * T], BF16)
                y_b = sbA("y_b", [128, FC * T], BF16)
                pstat = [psA("pstat%d" % i, [128, 512]) for i in (0, 1)]
                py = [psA("py%d" % i, [128, 512]) for i in (0, 1)]
                pz = [psA("pz%d" % i, [128, 512]) for i in (0, 1)]
                lds = [kb.stream("ldx%d" % i + ph) for i in (0, 1)]
                sts = [kb.stream("stx%d" % i + ph) for i in (0, 1)]
                srcv, dstv = fm(src_dram), fm(dst_dram)

                def load(t):
                    lds[t % 2].dma(c3(x_t[t % 2][:]), srcv[:, :, t * T:(t + 1) * T])

                def p1a(t):
                    x = x_t[t % 2]
                    lds[t % 2].wait_all("POOL")
                    mean, rstd = ln_stats(x[:], xb, sq, pstat[0], 1)
                    kb.op("DVE", lambda e: e.tensor_tensor(
                        out=c3(u_t[:]), in0=c3(x[:]), in1=bc(mean[:]),
                        op=ALU.subtract), [x[:], mean[:]], [u_t[:]])
                    kb.op("DVE", lambda e: e.tensor_tensor(
                        out=c3(u_t[:]), in0=c3(u_t[:]), in1=bc(rstd[:]),
                        op=ALU.mult), [u_t[:], rstd[:]], [u_t[:]])

                def p1b(t):
                    x = x_t[t % 2]
                    hb = h_b[t % 2]
                    kb.op("DVE", lambda e: e.tensor_mul(out=sq[:], in0=u_t[:],
                                                        in1=u_t[:]),
                          [u_t[:]], [sq[:]])
                    mean2, rstd2 = ln_stats_pre(u_t, sq, pstat[1], 2)
                    kb.op("DVE", lambda e: e.tensor_tensor(
                        out=c3(hf[:]), in0=c3(u_t[:]), in1=bc(mean2[:]),
                        op=ALU.subtract), [u_t[:], mean2[:]], [hf[:]])
                    kb.op("DVE", lambda e: e.tensor_tensor(
                        out=c3(hf[:]), in0=c3(hf[:]), in1=bc(rstd2[:]),
                        op=ALU.mult), [hf[:], rstd2[:]], [hf[:]])
                    for c in range(NC):
                        cs = slice(c * T, (c + 1) * T)
                        kb.op("DVE", lambda e, c=c, cs=cs: e.tensor_scalar(
                            out=hb[:, cs], in0=hf[:, cs],
                            scalar1=lgam_c[:, c:c + 1], scalar2=lbet_c[:, c:c + 1],
                            op0=ALU.mult, op1=ALU.add),
                            [hf[:], lgam_c[:], lbet_c[:]], [hb[:]])
                    kb.op("DVE", lambda e: e.scalar_tensor_tensor(
                        out=x_t[t % 2][:], in0=u_t[:], scalar=0.5, in1=x[:],
                        op0=ALU.mult, op1=ALU.add), [u_t[:], x[:]], [x[:]])

                def p2y(t):
                    hb = h_b[t % 2]
                    for m in range(FC):
                        p = py[m % 2]
                        kb.mms([(p[:, 0:T],
                                 w1b[:, c * F + m * 128:c * F + (m + 1) * 128],
                                 hb[:, c * T:(c + 1) * T], c == 0, c == NC - 1)
                                for c in range(NC)])
                        kb.op("ACT", lambda e, p=p, m=m: e.activation(
                            out=y_b[:, m * T:(m + 1) * T], in_=p[:, 0:T],
                            func=AF.Silu, bias=b1_c[:, m:m + 1], scale=1.0),
                            [p[:], b1_c[:]], [y_b[:]])

                def p2z(t, ms):
                    x = x_t[t % 2]
                    for m in ms:
                        p = pz[m % 2]
                        kb.mms([(p[:, 0:T],
                                 w2b[:, f * D + m * 128:f * D + (m + 1) * 128],
                                 y_b[:, f * T:(f + 1) * T], f == 0, f == FC - 1)
                                for f in range(FC)])
                        kb.op("DVE", lambda e, p=p, m=m: e.scalar_tensor_tensor(
                            out=x[:, m * T:(m + 1) * T], in0=p[:, 0:T],
                            scalar=b2_c[:, m:m + 1], in1=x[:, m * T:(m + 1) * T],
                            op0=ALU.add, op1=ALU.add),
                            [p[:], b2_c[:], x[:]], [x[:]])

                load(0)
                p1a(0)
                p1b(0)
                for t in range(nT):
                    if t + 1 < nT:
                        load(t + 1)
                    p2y(t)
                    if t + 1 < nT:
                        p1a(t + 1)
                    p2z(t, range(0, 4))
                    if t + 1 < nT:
                        p1b(t + 1)
                    p2z(t, range(4, NC))
                    sts[t % 2].dma(dstv[:, :, t * T:(t + 1) * T], c3(x_t[t % 2][:]))
                for s in sts:
                    s.wait_all("SP")

        esW = contextlib.ExitStack()
        with esW:
            w1b = esW.enter_context(nc.sbuf_tensor("w1b", [128, NC * F], BF16))
            w2b = esW.enter_context(nc.sbuf_tensor("w2b", [128, FC * D], BF16))
            with contextlib.ExitStack() as esD:
                sqi = [esD.enter_context(nc.sbuf_tensor("dq_i%d" % i, [128, F], I32))
                       for i in (0, 1)]
                scb = [esD.enter_context(nc.sbuf_tensor("dq_s%d" % i, [128, F], F32))
                       for i in (0, 1)]
                ldq = [kb.stream("ldq%d" % i) for i in (0, 1)]
                stw = kb.stream("stw")

                def dq_load(n, jobs):
                    qd, sd, c, width, *_ = jobs[n]
                    i = n % 2
                    ldq[i].dma(sqi[i][:, 0:width], qd[c * 128:(c + 1) * 128, :])
                    ldq[i].dma(scb[i][:, 0:width],
                               sd[c:c + 1, :].partition_broadcast(128))

                def dq_proc(n, jobs):
                    qd, sd, c, width, wout, wdram, quarter = jobs[n]
                    i = n % 2
                    ldq[i].wait_all("DVE")
                    if quarter:
                        kb.op("DVE", lambda e, i=i, width=width:
                              e.tensor_scalar_mul(
                                  out=scb[i][:, 0:width], in0=scb[i][:, 0:width],
                                  scalar1=0.25), [scb[i][:]], [scb[i][:]])
                    kb.op("DVE", lambda e, i=i, c=c, width=width:
                          e.scalar_tensor_tensor(
                              out=wout[:, c * width:(c + 1) * width],
                              in0=sqi[i][:, 0:width], scalar=128.0,
                              in1=scb[i][:, 0:width],
                              op0=ALU.subtract, op1=ALU.mult),
                          [sqi[i][:], scb[i][:]], [wout[:]])
                    stw.dma(wdram[:, c * width:(c + 1) * width],
                            wout[:, c * width:(c + 1) * width])

                jobs = ([(fc1q, fc1s, c, F, w1b, w1d, False) for c in range(NC)]
                        + [(fc2q, fc2s, c, D, w2b, w2d, True)
                           for c in range(FC)])
                dq_load(0, jobs)
                for n in range(len(jobs)):
                    if n + 1 < len(jobs):
                        dq_load(n + 1, jobs)
                    dq_proc(n, jobs)
                stw.wait_all("SP")
            ffn_phase(xT, x1d, w1b, w2b, "A")
        kb.barrier()

        esKT = contextlib.ExitStack()
        with esKT:
            kTs = esKT.enter_context(nc.sbuf_tensor("kTs", [128, NC * S], BF16))
            esB = contextlib.ExitStack()
            with esB:
                sbB = lambda n, sh, dt: esB.enter_context(nc.sbuf_tensor(n, sh, dt))
                psB = lambda n, sh: esB.enter_context(nc.psum_tensor(n, sh, F32))
                Wqb = sbB("Wqb", [128, NC * D], BF16)
                Wkb = sbB("Wkb", [128, NC * D], BF16)
                Wvb = sbB("Wvb", [128, NC * D], BF16)
                bvb = sbB("bvb", [128, D], F32)
                wtmp = [sbB("wtmp%d" % i, [128, D], F32) for i in (0, 1)]
                x_t = [sbB("bx_t%d" % i, [128, NC * T], F32) for i in (0, 1)]
                u2_t = sbB("bu2", [128, NC * T], F32)
                u2b = [sbB("bu2b%d" % i, [128, NC * T], BF16) for i in (0, 1)]
                xb = sbB("bxb", [128, NC * T], BF16)
                sq = sbB("bsq", [128, NC * T], BF16)
                q_b = sbB("bq_b", [128, NC * T], BF16)
                v_sb = sbB("bv_sb", [128, 1040], BF16)
                qsq = sbB("bqsq", [128, NC * T], BF16)
                ksq = sbB("bksq", [128, NC * T], BF16)
                n2m = sbB("bn2m", [16, 1], F32)
                pstat = [psB("bpstat%d" % i, [128, 512]) for i in (0, 1)]
                pp = [psB("bpp%d" % i, [128, 512]) for i in (0, 1)]
                pv = [psB("bpv%d" % i, [128, 512]) for i in (0, 1)]
                pqn = psB("bpqn", [16, 512])
                pkn = psB("bpkn", [16, 512])
                ldw = kb.stream("ldwB")
                lds = [kb.stream("ldb%d" % i) for i in (0, 1)]
                sts = [kb.stream("stb%d" % i) for i in (0, 1)]

                wjobs = [(Wd, Wsb, c) for Wd, Wsb in
                         ((Wq, Wqb), (Wk, Wkb), (Wv, Wvb)) for c in range(NC)]
                ldww = [kb.stream("ldww%d" % i) for i in (0, 1)]
                ldww[0].dma(wtmp[0][:], wjobs[0][0][0:128, :])
                for n, (Wd, Wsb, c) in enumerate(wjobs):
                    if n + 1 < len(wjobs):
                        W2, _, c2 = wjobs[n + 1]
                        ldww[(n + 1) % 2].dma(wtmp[(n + 1) % 2][:],
                                              W2[c2 * 128:(c2 + 1) * 128, :])
                    ldww[n % 2].wait_all("DVE")
                    w = wtmp[n % 2]
                    kb.op("DVE", lambda e, w=w, Wsb=Wsb, c=c: e.tensor_copy(
                        out=Wsb[:, c * D:(c + 1) * D], in_=w[:]),
                        [w[:]], [Wsb[:]])
                ldw.dma(bvb[:], bv.rearrange("(o d) -> o d", o=1)
                        .partition_broadcast(128))

                kb.op("DVE", lambda e: e.memset(v_sb[:], 1.0), [], [v_sb[:]])
                x1v, u2v, qv = fm(x1d), fm(u2d), fm(qTd)
                kv3 = kTs.ap().rearrange("p (c s) -> p c s", s=S)

                def loadB(t):
                    lds[t % 2].dma(c3(x_t[t % 2][:]), x1v[:, :, t * T:(t + 1) * T])

                def bp1cast(t):
                    x = x_t[t % 2]
                    lds[t % 2].wait_all("DVE")
                    kb.op("DVE", lambda e, x=x: e.tensor_copy(out=xb[:], in_=x[:]),
                          [x[:]], [xb[:]])
                    kb.op("DVE", lambda e: e.tensor_mul(out=sq[:], in0=xb[:],
                                                        in1=xb[:]),
                          [xb[:]], [sq[:]])

                def bp1(t):
                    x = x_t[t % 2]
                    ub = u2b[t % 2]
                    mean, rstd = ln_stats_pre(xb, sq, pstat[0], 1)
                    kb.op("DVE", lambda e: e.tensor_tensor(
                        out=c3(u2_t[:]), in0=c3(x[:]), in1=bc(mean[:]),
                        op=ALU.subtract), [x[:], mean[:]], [u2_t[:]])
                    kb.op("DVE", lambda e, ub=ub: e.tensor_tensor(
                        out=c3(ub[:]), in0=c3(u2_t[:]), in1=bc(rstd[:]),
                        op=ALU.mult), [u2_t[:], rstd[:]], [ub[:]])
                    sts[t % 2].dma(u2v[:, :, t * T:(t + 1) * T], c3(ub[:]))

                def bp2q(t):
                    ub = u2b[t % 2]
                    for m in range(NC):
                        p = pp[m % 2]
                        kb.mms([(p[:, 0:T],
                                 Wqb[:, c * D + m * 128:c * D + (m + 1) * 128],
                                 ub[:, c * T:(c + 1) * T], c == 0, c == NC - 1)
                                for c in range(NC)])
                        kb.op("ACT", lambda e, p=p, m=m: e.activation(
                            out=q_b[:, m * T:(m + 1) * T], in_=p[:, 0:T],
                            func=AF.Identity, bias=bq_c[:, m:m + 1], scale=1.0),
                            [p[:], bq_c[:]], [q_b[:]])
                    sts[t % 2].dma(qv[:, :, t * T:(t + 1) * T], c3(q_b[:]))
                    kb.op("DVE", lambda e: e.tensor_mul(
                        out=qsq[:], in0=q_b[:], in1=q_b[:]), [q_b[:]], [qsq[:]])

                def bp2k(t):
                    ub = u2b[t % 2]
                    for m in range(NC):
                        p = pp[m % 2]
                        kb.mms([(p[:, 0:T],
                                 Wkb[:, c * D + m * 128:c * D + (m + 1) * 128],
                                 ub[:, c * T:(c + 1) * T], c == 0, c == NC - 1)
                                for c in range(NC)])
                        kb.op("ACT", lambda e, p=p, m=m, t=t: e.activation(
                            out=kTs[:, m * S + t * T:m * S + (t + 1) * T],
                            in_=p[:, 0:T], func=AF.Identity,
                            bias=bk_c[:, m:m + 1], scale=1.0),
                            [p[:], bk_c[:]], [kTs[:]])
                    kb.op("DVE", lambda e, t=t: e.tensor_mul(
                        out=c3(ksq[:]), in0=kv3[:, :, t * T:(t + 1) * T],
                        in1=kv3[:, :, t * T:(t + 1) * T]), [kTs[:]], [ksq[:]])

                def bp2v(t):
                    ub = u2b[t % 2]
                    for sub in range(T // 128):
                        for dvc in range(2):
                            p = pv[dvc]
                            kb.mms([(p[:, 0:512],
                                     ub[:, c * T + sub * 128:c * T + (sub + 1) * 128],
                                     Wvb[:, c * D + dvc * 512:c * D + (dvc + 1) * 512],
                                     c == 0, c == NC - 1) for c in range(NC)])
                            vout = v_sb.ap()[:, dvc * 520:dvc * 520 + 520] \
                                .rearrange("p (h c) -> p h c", c=65)[:, 0:8, 0:64]
                            kb.op("DVE", lambda e, p=p, dvc=dvc, vout=vout:
                                  e.tensor_tensor(
                                      out=vout,
                                      in0=p.ap()[:, 0:512].rearrange(
                                          "p (h c) -> p h c", c=64),
                                      in1=bvb.ap()[:, dvc * 512:(dvc + 1) * 512]
                                      .rearrange("p (h c) -> p h c", c=64),
                                      op=ALU.add),
                                  [p[:], bvb[:]], [v_sb[:]])
                        row0 = t * T + sub * 128
                        sts[t % 2].dma(vd[row0:row0 + 128, :], v_sb[:])

                def bp2sel(t):
                    for srcb, prun, run in ((qsq, pqn, qn2run), (ksq, pkn, kn2run)):
                        kb.mms([(prun[0:16, 0:T], sel_sb[:, c * 16:(c + 1) * 16],
                                 srcb[:, c * T:(c + 1) * T], c == 0, c == NC - 1)
                                for c in range(NC)])
                        kb.op("DVE", lambda e, prun=prun: e.reduce_max(
                            out=n2m[:], in_=prun[0:16, 0:T], axis=AX.X),
                            [prun[:]], [n2m[:]])
                        if t == 0:
                            kb.op("DVE", lambda e, run=run: e.tensor_copy(
                                out=run[:], in_=n2m[:]), [n2m[:]], [run[:]])
                        else:
                            kb.op("DVE", lambda e, run=run: e.tensor_tensor(
                                out=run[:], in0=run[:], in1=n2m[:], op=ALU.max),
                                [run[:], n2m[:]], [run[:]])

                loadB(0)
                bp1cast(0)
                bp1(0)
                for t in range(nT):
                    if t + 1 < nT:
                        loadB(t + 1)
                    bp2q(t)
                    if t + 1 < nT:
                        bp1cast(t + 1)
                    bp2k(t)
                    if t + 1 < nT:
                        bp1(t + 1)
                    bp2v(t)
                    bp2sel(t)
                kb.op("DVE", lambda e: e.tensor_mul(out=qn2run[:], in0=qn2run[:],
                                                    in1=kn2run[:]),
                      [qn2run[:], kn2run[:]], [qn2run[:]])
                kb.op("ACT", lambda e: e.activation(out=qn2run[:], in_=qn2run[:],
                                                    func=AF.Sqrt, bias=0.0,
                                                    scale=1.0 / 64.0),
                      [qn2run[:]], [qn2run[:]])
                kb.op("PE", lambda e: e.transpose(out=pstat[0][0:1, 0:16],
                                                  in_=qn2run[:],
                                                  identity=id16_sb[:]),
                      [qn2run[:], id16_sb[:]], [pstat[0][:]])
                kb.op("DVE", lambda e: e.tensor_scalar_mul(
                    out=negbrow[:], in0=pstat[0][0:1, 0:16], scalar1=-1.0),
                    [pstat[0][:]], [negbrow[:]])
                kb.op("DVE", lambda e: e.tensor_tensor(
                    out=negbrow[:, 0:8], in0=negbrow[0:1, 0:16:2],
                    in1=negbrow[0:1, 1:16:2], op=ALU.min),
                    [negbrow[:]], [negbrow[:]])
                kb.mms([(pstat[1][0:128, 0:8], ones_f[:], negbrow[:, 0:8],
                         True, True)])
                kb.op("DVE", lambda e: e.tensor_copy(out=negb_c[:],
                                                     in_=pstat[1][0:128, 0:8]),
                      [pstat[1][:]], [negb_c[:]])
                for s in sts:
                    s.wait_all("SP")
            kb.barrier()

            esT = contextlib.ExitStack()
            with esT:
                sbT = lambda n, sh, dt: esT.enter_context(nc.sbuf_tensor(n, sh, dt))
                psT = lambda n, sh: esT.enter_context(nc.psum_tensor(n, sh, F32))
                v65 = sbT("v65", [128, nKB * 1040], BF16)
                qTg = [sbT("qTg%d" % i, [128, NC * QG], BF16) for i in (0, 1)]
                PT = [sbT("PT%d" % i, [128, 1024], BF16) for i in (0, 1)]
                oTg = [sbT("oTg%d" % i, [128, NC * QG], BF16) for i in (0, 1)]
                ssb = [sbT("ssb%d" % i, [65, 1024], F32) for i in (0, 1)]
                pS = [psT("pS%d" % i, [128, 1024]) for i in (0, 1)]
                pAV = [psT("pAV%d" % i, [128, 512]) for i in range(4)]
                ldv = kb.stream("ldv")
                ldqg = [kb.stream("ldqg%d" % i) for i in (0, 1)]
                sto = kb.stream("sto")

                vsrc = vd.rearrange("(b p) c -> p b c", p=128)
                vdst = v65.ap().rearrange("p (b c) -> p b c", c=1040)
                for j in range(4):
                    blk = nKB // 4
                    ldv.dma(vdst[:, j * blk:(j + 1) * blk, :],
                            vsrc[:, j * blk:(j + 1) * blk, :])
                qgv = fm(qTd)

                def loadG(g):
                    ldqg[g % 2].dma(
                        qTg[g % 2].ap().rearrange("p (c s) -> p c s", s=QG),
                        qgv[:, :, g * QG:(g + 1) * QG])

                loadG(0)
                for g in range(nG):
                    if g + 1 < nG:
                        loadG(g + 1)
                    qg = qTg[g % 2]
                    ldqg[g % 2].wait_all("PE")
                    ldv.wait_all("PE")
                    kbmax = 4 * g + 3
                    units = [(c, kbi) for c in range(8)
                             for kbi in range(kbmax + 1)]

                    def emit_S(idx):
                        c, kbi = units[idx]
                        p = pS[idx % 2]
                        lA = kTs[0:64, c * S + kbi * 128:c * S + (kbi + 1) * 128]
                        lB = kTs[64:128, c * S + kbi * 128:c * S + (kbi + 1) * 128]
                        rA = qg[0:64, c * QG:(c + 1) * QG]
                        rB = qg[64:128, c * QG:(c + 1) * QG]
                        m = kbi - 4 * g
                        items = []
                        for off, l, r in ((0, lA, rA), (512, lB, rB)):
                            if 0 <= m:
                                items += [(p[:, off:off + 512], l, r,
                                           True, False),
                                          (p[:, off + m * 128:
                                             off + (m + 1) * 128],
                                           id128_sb[:], mask_sb[:],
                                           False, True)]
                            else:
                                items.append((p[:, off:off + 512], l, r,
                                              True, True))
                        kb.mms(items)

                    def emit_exp(idx):
                        c, kbi = units[idx]
                        p = pS[idx % 2]
                        pt = PT[idx % 2]
                        kb.op("ACT", lambda e, p=p, pt=pt, c=c: e.activation(
                            out=pt[:], in_=p[:], func=AF.Exp,
                            bias=negb_c[:, c:c + 1], scale=0.125),
                            [p[:], negb_c[:]], [pt[:]])

                    def emit_AV(idx):
                        c, kbi = units[idx]
                        pt = PT[idx % 2]
                        pavA, pavB = pAV[2 * (c % 2)], pAV[2 * (c % 2) + 1]
                        hA, hB = 2 * c, 2 * c + 1
                        m = max(kbi - 4 * g, 0)
                        q0 = m * 128
                        vA = v65[:, kbi * 1040 + hA * 65:
                                 kbi * 1040 + hA * 65 + 65]
                        vB = v65[:, kbi * 1040 + hB * 65:
                                 kbi * 1040 + hB * 65 + 65]
                        kb.mms([(pavA[0:65, q0:QG], vA, pt[:, q0:512],
                                 kbi == 0, kbi == kbmax),
                                (pavB[0:65, q0:QG], vB, pt[:, 512 + q0:1024],
                                 kbi == 0, kbi == kbmax)])

                    def emit_tail(c):
                        og = oTg[g % 2]
                        sb_ = ssb[c % 2]
                        pavA, pavB = pAV[2 * (c % 2)], pAV[2 * (c % 2) + 1]
                        for hh, pav in ((0, pavA), (1, pavB)):
                            rows = slice(0, 64) if hh == 0 else slice(64, 128)
                            kb.op("DVE", lambda e, pav=pav, rows=rows, c=c,
                                  og=og: e.tensor_copy(
                                      out=og[rows, c * QG:(c + 1) * QG],
                                      in_=pav[0:64, 0:QG]),
                                  [pav[:]], [og[:]])
                            kb.op("DVE", lambda e, pav=pav, sb_=sb_, hh=hh:
                                  e.tensor_copy(
                                      out=sb_[64:65, hh * QG:(hh + 1) * QG],
                                      in_=pav[64:65, 0:QG]),
                                  [pav[:]], [sb_[:]])
                            sto.dma(sumd[2 * c + hh:2 * c + hh + 1,
                                         g * QG:(g + 1) * QG],
                                    sb_[64:65, hh * QG:(hh + 1) * QG])

                    emit_S(0)
                    for idx in range(len(units)):
                        emit_exp(idx)
                        if idx + 1 < len(units):
                            emit_S(idx + 1)
                        emit_AV(idx)
                        c, kbi = units[idx]
                        if kbi == kbmax:
                            emit_tail(c)
                    sto.dma(fm(oTd)[:, :, g * QG:(g + 1) * QG],
                            oTg[g % 2].ap().rearrange("p (c s) -> p c s", s=QG))
                sto.wait_all("SP")
        kb.barrier()

        esE = contextlib.ExitStack()
        with esE:
            sbE = lambda n, sh, dt: esE.enter_context(nc.sbuf_tensor(n, sh, dt))
            psE = lambda n, sh: esE.enter_context(nc.psum_tensor(n, sh, F32))
            Wob = sbE("Wob", [128, NC * D], BF16)
            wtmp = [sbE("ewtmp%d" % i, [128, D], F32) for i in (0, 1)]
            o_t = [sbE("eo_t%d" % i, [128, NC * T], BF16) for i in (0, 1)]
            u2_t = [sbE("eu2%d" % i, [128, NC * T], BF16) for i in (0, 1)]
            x1_t = [sbE("ex1%d" % i, [128, NC * T], F32) for i in (0, 1)]
            y_t = [sbE("ey_t%d" % i, [128, NC * T], F32) for i in (0, 1)]
            xb = sbE("exb", [128, NC * T], BF16)
            sq = sbE("esq", [128, NC * T], BF16)
            ssb2 = sbE("essum", [16, S], F32)
            rr_b = sbE("err_b", [16, S], BF16)
            pstat = [psE("epstat%d" % i, [128, 512]) for i in (0, 1)]
            po = [psE("epo%d" % i, [128, 512]) for i in (0, 1)]
            pR = [psE("epR%d" % i, [128, 512]) for i in (0, 1)]
            ldw = kb.stream("ldwE")
            lds = [kb.stream("lde%d" % i) for i in (0, 1)]
            sts = [kb.stream("ste%d" % i) for i in (0, 1)]

            ldsm = kb.stream("ldsm")
            ldsm.dma(ssb2[:], sumd)
            ldsm.wait_all("DVE")
            kb.op("DVE", lambda e: e.reciprocal(out=ssb2[:], in_=ssb2[:]),
                  [ssb2[:]], [ssb2[:]])
            kb.op("DVE", lambda e: e.tensor_copy(out=rr_b[:], in_=ssb2[:]),
                  [ssb2[:]], [rr_b[:]])
            ldwe = [kb.stream("ldwe%d" % i) for i in (0, 1)]
            ldwe[0].dma(wtmp[0][:], Wo[0:128, :])
            for c in range(NC):
                if c + 1 < NC:
                    ldwe[(c + 1) % 2].dma(wtmp[(c + 1) % 2][:],
                                          Wo[(c + 1) * 128:(c + 2) * 128, :])
                ldwe[c % 2].wait_all("DVE")
                w = wtmp[c % 2]
                kb.op("DVE", lambda e, w=w, c=c: e.tensor_copy(
                    out=Wob[:, c * D:(c + 1) * D], in_=w[:]), [w[:]], [Wob[:]])
            ov, u2v, x1v, x2v = fm(oTd), fm(u2d), fm(x1d), fm(x2d)

            def loadE(t):
                i = t % 2
                lds[i].dma(c3(o_t[i][:]), ov[:, :, t * T:(t + 1) * T])
                lds[i].dma(c3(u2_t[i][:]), u2v[:, :, t * T:(t + 1) * T])
                lds[i].dma(c3(x1_t[i][:]), x1v[:, :, t * T:(t + 1) * T])


            def ep1(t):
                i = t % 2
                lds[i].wait_all("PE")
                lds[i].wait_all("DVE")
                for m in range(NC):
                    pr = pR[m % 2]
                    kb.mms([(pr[:, 0:T], selh_sb[:, m * 128:(m + 1) * 128],
                             rr_b[0:16, t * T:(t + 1) * T], True, True)])
                    kb.op("DVE", lambda e, i=i, m=m, pr=pr: e.tensor_mul(
                        out=o_t[i][:, m * T:(m + 1) * T],
                        in0=o_t[i][:, m * T:(m + 1) * T], in1=pr[:, 0:T]),
                        [o_t[i][:], pr[:]], [o_t[i][:]])
                for m in range(NC):
                    p = po[m % 2]
                    kb.mms([(p[:, 0:T],
                             Wob[:, c * D + m * 128:c * D + (m + 1) * 128],
                             o_t[i][:, c * T:(c + 1) * T], c == 0, c == NC - 1)
                            for c in range(NC)])
                    kb.op("DVE", lambda e, p=p, m=m, i=i: e.scalar_tensor_tensor(
                        out=y_t[i][:, m * T:(m + 1) * T], in0=p[:, 0:T],
                        scalar=bo_c[:, m:m + 1], in1=u2_t[i][:, m * T:(m + 1) * T],
                        op0=ALU.add, op1=ALU.add),
                        [p[:], bo_c[:], u2_t[i][:]], [y_t[i][:]])

            def ep2(t):
                i = t % 2
                y, x1 = y_t[i], x1_t[i]
                mean, rstd = ln_stats(y[:], xb, sq, pstat[0], 1)
                kb.op("POOL", lambda e, y=y: e.tensor_tensor(
                    out=c3(y[:]), in0=c3(y[:]), in1=bc(mean[:]),
                    op=ALU.subtract), [y[:], mean[:]], [y[:]])
                kb.op("POOL", lambda e, y=y: e.tensor_tensor(
                    out=c3(y[:]), in0=c3(y[:]), in1=bc(rstd[:]),
                    op=ALU.mult), [y[:], rstd[:]], [y[:]])
                for m in range(NC):
                    ms = slice(m * T, (m + 1) * T)
                    kb.op("DVE", lambda e, m=m, ms=ms, x1=x1, y=y:
                          e.scalar_tensor_tensor(
                              out=x1[:, ms], in0=y[:, ms],
                              scalar=agam_c[:, m:m + 1], in1=x1[:, ms],
                              op0=ALU.mult, op1=ALU.add),
                          [y[:], agam_c[:], x1[:]], [x1[:]])
                    kb.op("DVE", lambda e, m=m, ms=ms, x1=x1: e.tensor_scalar(
                        out=x1[:, ms], in0=x1[:, ms], scalar1=abet_c[:, m:m + 1],
                        scalar2=None, op0=ALU.add),
                        [x1[:], abet_c[:]], [x1[:]])
                sts[i].dma(x2v[:, :, t * T:(t + 1) * T], c3(x1[:]))

            loadE(0)
            ep1(0)
            for t in range(nT):
                if t + 1 < nT:
                    loadE(t + 1)
                ep2(t)
                if t + 1 < nT:
                    ep1(t + 1)
            for s in sts:
                s.wait_all("SP")
        kb.barrier()

        esC = contextlib.ExitStack()
        with esC:
            w1b = esC.enter_context(nc.sbuf_tensor("w1bC", [128, NC * F], BF16))
            w2b = esC.enter_context(nc.sbuf_tensor("w2bC", [128, FC * D], BF16))
            ldw = kb.stream("ldwC")
            for j in range(4):
                w = NC * F // 4
                ldw.dma(w1b[:, j * w:(j + 1) * w], w1d[:, j * w:(j + 1) * w])
                ldw.dma(w2b[:, j * w:(j + 1) * w], w2d[:, j * w:(j + 1) * w])
            ldw.wait_all("PE")
            ffn_phase(x2d, outT, w1b, w2b, "C")
    return nc


def kernel(**inputs):
    S = 4096
    x = np.asarray(inputs["L0_attn_x"], dtype=np.float32)
    nm = {
        "Wq": "L0_attn_Wq", "bq": "L0_attn_bq", "Wk": "L0_attn_Wk",
        "bk": "L0_attn_bk", "Wv": "L0_attn_Wv", "bv": "L0_attn_bv",
        "Wo": "L0_attn_Wo", "bo": "L0_attn_bo", "agam": "L0_attn_gamma",
        "abet": "L0_attn_beta", "lgam": "L0_ffn_ln_gamma",
        "lbet": "L0_ffn_ln_beta", "fc1s": "L0_ffn_fc1_scales",
        "fc1b": "L0_ffn_fc1_bias", "fc2s": "L0_ffn_fc2_scales",
        "fc2b": "L0_ffn_fc2_bias",
    }
    base = {k: np.ascontiguousarray(np.asarray(inputs[v], dtype=np.float32))
            for k, v in nm.items()}
    base["fc1q"] = np.ascontiguousarray(np.asarray(inputs["L0_ffn_fc1_q"],
                                                   dtype=np.int32))
    base["fc2q"] = np.ascontiguousarray(np.asarray(inputs["L0_ffn_fc2_q"],
                                                   dtype=np.int32))
    in_maps = []
    for b in range(8):
        m = dict(base)
        m["xT"] = np.ascontiguousarray(x[b].T)
        in_maps.append(m)
    nc = build_nc(S)
    res = run_bass_kernel_spmd(nc, in_maps, core_ids=list(range(8)))
    out = np.stack([np.asarray(r["outT"]).T for r in res.results], axis=0)
    return out.astype(np.float32)
